# version 11
# baseline (speedup 1.0000x reference)
import numpy as np
import concourse.bass as bass
import concourse.mybir as mybir
from concourse.bass_utils import run_bass_kernel_spmd
from concourse.bass_core_rust import IndirectOffsetOnAxis
from contextlib import ExitStack

F32 = mybir.dt.float32
BF16 = mybir.dt.bfloat16
U32 = mybir.dt.uint32
AF = mybir.ActivationFunctionType
ALU = mybir.AluOpType
AX = mybir.AxisListType

NCORES = 8


def cfg_full():
    return dict(DM=4096, SEQ=2048, DIN=8192, R=256, N=16, K=4, E=8, DK=32, ER=128)


def cfg_small():
    return dict(DM=512, SEQ=256, DIN=1024, R=256, N=16, K=4, E=8, DK=32, ER=128)


def derive(c):
    d = dict(c)
    d["T"] = c["SEQ"] // 2
    d["DH"] = c["DIN"] // 2
    d["NK"] = c["DM"] // 128
    d["NDT"] = d["DH"] // 128
    d["NTS"] = d["T"] // 128
    d["TW"] = min(512, d["T"])
    d["NTW"] = d["T"] // d["TW"]
    d["TC"] = min(256, c["SEQ"] // 2)
    d["NTC"] = c["SEQ"] // d["TC"]
    d["NRC"] = c["R"] // 128
    d["NQC"] = (c["E"] * c["DK"]) // 128
    return d


class Sem:
    def __init__(self, h):
        self.h = h
        self.n = 0


class P:

    def __init__(self, nc, role, eng, sems):
        self.nc, self.role, self.eng, self.sems = nc, role, eng, sems

    def _emit(self, _role, _opname, *a, **kw):
        if self.role == _role:
            return getattr(self.eng, _opname)(*a, **kw)
        return None

    def op(self, _role, _opname, *a, sem=None, inc=1, **kw):
        ins = self._emit(_role, _opname, *a, **kw)
        if sem is not None:
            sem.n += inc
            if ins is not None:
                ins.then_inc(sem.h, inc)
        return ins

    def v(self, opname, *a, sem=None, **kw):
        return self.op("vector", opname, *a, sem=sem, **kw)

    def a(self, opname, *a, sem=None, **kw):
        return self.op("scalar", opname, *a, sem=sem, **kw)

    def t(self, opname, *a, sem=None, **kw):
        return self.op("tensor", opname, *a, sem=sem, **kw)

    def p(self, opname, *a, sem=None, **kw):
        return self.op("gpsimd", opname, *a, sem=sem, **kw)

    def ydma(self, dst, src, sem):
        return self.op("sync", "dma_start", sem=sem, inc=16, out=dst, in_=src)

    def pdma(self, dst, src, sem):
        return self.op("gpsimd", "dma_start", sem=sem, inc=16, out=dst, in_=src)

    def pgather(self, dst, src, idxap, eoff, sem):
        ins = None
        if self.role == "gpsimd":
            ins = self.eng.indirect_dma_start(
                out=dst, out_offset=None, in_=src,
                in_offset=IndirectOffsetOnAxis(ap=idxap, axis=0),
                element_offset=eoff)
        sem.n += 16
        if ins is not None:
            ins.then_inc(sem.h, 16)
        return ins

    def wait(self, role, sem, val=None):
        if self.role == role:
            self.eng.wait_ge(sem.h, sem.n if val is None else val)

    def waitall(self, sem, val=None):
        self.eng.wait_ge(sem.h, sem.n if val is None else val)

    def drain(self, role, sem=None):
        ins = self._emit(role, "drain")
        if sem is not None:
            sem.n += 1
            if ins is not None:
                ins.then_inc(sem.h, 1)
        return ins

    def barrier(self):
        for s in ("L", "L2", "LB", "G", "G2", "GS", "GS2"):
            if self.sems[s].n:
                self.waitall(self.sems[s])
        for role, s in (("vector", "BV"), ("scalar", "BA"),
                        ("tensor", "BT"), ("gpsimd", "BP")):
            self.drain(role, self.sems[s])
        for s in ("BV", "BA", "BT", "BP"):
            self.waitall(self.sems[s])


def build_program(C, taps=()):
    c = C
    DM, SEQ, DIN, R, N, K = c["DM"], c["SEQ"], c["DIN"], c["R"], c["N"], c["K"]
    E, DK, ER = c["E"], c["DK"], c["ER"]
    T, DH, NK, NDT, NTS = c["T"], c["DH"], c["NK"], c["NDT"], c["NTS"]
    TW, NTW, TC, NTC, NRC, NQC = c["TW"], c["NTW"], c["TC"], c["NTC"], c["NRC"], c["NQC"]
    EDK = E * DK
    NTC_H = NTC // 2
    BCW = 2 * N
    RQ = R + 2 * N
    nc = bass.Bass("TRN2", target_bir_lowering=False)

    def din(name, shape, dt=F32):
        return nc.dram_tensor(name, shape, dt, kind="ExternalInput").ap()

    x0 = din("x0", [NK, 128, T])
    ffn_wt = din("ffn_wt", [DM, DM])
    ffn_b = din("ffn_b", [128, NK])
    wq_t = din("wq_t", [DM, EDK])
    wk_t = din("wk_t", [DM, EDK])
    af_t = din("af_t", [DM, ER])
    bm2 = din("bm2", [ER, DM])
    ip_x = din("ip_x", [DM, DH])
    ip_z = din("ip_z", [DM, DH])
    convw = din("convw", [128, NDT * K])
    convb = din("convb", [128, NDT])
    xp_t = din("xp_t", [DH, RQ])
    dtp_t = din("dtp_t", [R, DH])
    dtb = din("dtb", [128, NDT])
    alog = din("alog", [128, NDT * N])
    dpar = din("dpar", [128, NDT])
    op_t = din("op_t", [DH, DM])
    dtln = din("dtln", [128, R])
    bcln = din("bcln", [128, 2 * N])
    ident = din("ident", [128, 128])
    ones128 = din("ones128", [128, 1])
    ones1 = din("ones1", [1, 128])
    blk = din("blk", [128, NQC * E])
    r8 = din("r8", [E, 128])
    ixnames = ["ix_rhs0", "ix_rhs1", "ix_pa", "ix_pb", "ix_own", "ix_gsrc", "ix_peer"]
    ixd = {nm: din(nm, [128, 1], U32) for nm in ixnames}

    out = nc.dram_tensor("out", [NK, 128, T], F32, kind="ExternalOutput").ap()
    tap_d = {nm: nc.dram_tensor("tap_" + nm, shp, F32, kind="ExternalOutput").ap()
             for nm, shp in taps}

    def dtmp(name, shape, dt=F32):
        return nc.dram_tensor(name, shape, dt).ap()

    x1d = dtmp("x1d", [NK, 128, T])
    x2d = dtmp("x2d", [NK, 128, T])
    xcd = dtmp("xcd", [2, NDT, 128, T])
    szd = dtmp("szd", [2, NDT, 128, T])
    yd = dtmp("yd", [NDT, 128, SEQ])
    pard = dtmp("pard", [2, NK, 128, T])
    bcd = dtmp("bcd", [BCW, SEQ])
    agx_in = dtmp("agx_in", [2, NK, 128, T], BF16)
    agx_out = dtmp("agx_out", [NCORES, 2, NK, 128, T], BF16)
    w16 = {nm: dtmp("w16_" + nm, [2] + shp, BF16) for nm, shp in [
        ("ffn", [DM, DM]), ("wq", [DM, EDK]), ("wk", [DM, EDK]),
        ("af", [DM, ER]), ("bm2", [ER, DM]),
        ("ipx", [DM, DH]), ("ipz", [DM, DH]), ("op", [DH, DM])]}
    agp_in = dtmp("agp_in", [2 * NTS, 128, RQ])
    agp_out = dtmp("agp_out", [NCORES, 2 * NTS, 128, RQ])
    agg_in = dtmp("agg_in", [NK, 128, T])
    agg_out = dtmp("agg_out", [NCORES, NK, 128, T])

    SCAN_DTRT = 0
    SCAN_NEED = NRC * SEQ + N * TC + 9 * TC + 512 + 4 * N * TC
    BIGW = max(NK * T, SCAN_NEED)
    WW = max(9 * T, 4 * R + 2 * BCW + 256 + 2 * N * TC)

    with ExitStack() as ctx:
        sb = lambda nm, shp, dt=F32: ctx.enter_context(nc.sbuf_tensor(nm, shp, dt))
        big = sb("big", [128, BIGW])
        slab = sb("slab", [128, 2 * NK * 128])
        work = sb("work", [128, WW])
        s_id = sb("s_id", [128, 128])
        s_ones128 = sb("s_ones128", [128, 1])
        s_ones1 = sb("s_ones1", [1, 128])
        s_blk = sb("s_blk", [128, NQC * E])
        s_r8 = sb("s_r8", [E, 128])
        s_halfb = sb("s_halfb", [128, NK])
        s_convw = sb("s_convw", [128, NDT * K])
        s_convb = sb("s_convb", [128, NDT])
        s_dtb = sb("s_dtb", [128, NDT])
        s_aneg = sb("s_aneg", [128, NDT * N])
        s_dpar = sb("s_dpar", [128, NDT])
        s_dtln = sb("s_dtln", [128, R])
        s_bcln = sb("s_bcln", [128, 2 * N])
        s_cexp = sb("s_cexp", [128, 1])
        s_ix = {nm: sb("s_" + nm, [128, 1], U32) for nm in ixnames}
        s_hst = sb("s_hst", [128, NDT * N])
        s_halo = sb("s_halo", [128, NDT * (K - 1)])
        s_xq = sb("s_xq", [128, 2 * RQ])

        PW = max(TW, RQ)
        pb = [ctx.enter_context(nc.psum_tensor(f"pb{i}", [128, PW], F32))
              for i in range(8)]

        semnames = ["L", "L2", "LB", "G", "G2", "GS", "GS2", "C", "SV", "SA", "ST", "SP",
                    "BV", "BA", "BT", "BP", "FR"]
        sems = {nm: Sem(ctx.enter_context(nc.semaphore(nm))) for nm in semnames}
        block = ctx.enter_context(nc.Block())

        def walk(role, eng):
            for s in sems.values():
                s.n = 0
            p = P(nc, role, eng, sems)
            L, G, C = sems["L"], sems["G"], sems["C"]
            L2, G2, GS = sems["L2"], sems["G2"], sems["GS"]
            GS2, LB = sems["GS2"], sems["LB"]
            GSp = lambda i: (GS, GS2)[i % 2]
            Lp = lambda i: (L, L2)[i % 2]
            Gp = lambda i: (G, G2)[i % 2]
            SV, SA, ST, SP, FR = (sems["SV"], sems["SA"], sems["ST"], sems["SP"],
                                  sems["FR"])
            W = work
            s_x0re = W[:, 7 * T:9 * T]
            s_bm2sl = W[:, 0:256]
            s_qsb = W[:, WW - TW:WW]
            s_w8 = W[0:E, WW - TW - T:WW - TW]
            s_bc = W[0:BCW, 4 * R:4 * R + SEQ]
            s_ssm = W[:, WW - 2 * NTS * RQ:WW]
            s_gt = W[:, 0:max(2 * T, 2 * RQ)]
            s_xa = W[:, 0:2 * T]
            s_xb = W[:, 2 * T:4 * T]
            s_xc2 = W[:, 4 * T:6 * T]
            big16 = big[:, :].bitcast(BF16)
            slab16 = slab[:, :].bitcast(BF16)
            bm16 = W[:, 0:256].bitcast(BF16)

            def xth(k, wo=0, ww=None):
                ww = T if ww is None else ww
                return big16[:, 2 * k * T + wo:2 * k * T + wo + ww]

            def xtl(k, wo=0, ww=None):
                ww = T if ww is None else ww
                return big16[:, (2 * k + 1) * T + wo:(2 * k + 1) * T + wo + ww]

            def tap(name, src_sbuf):
                if name in tap_d:
                    p.wait("gpsimd", SV)
                    p.wait("gpsimd", SA)
                    p.wait("gpsimd", SP)
                    p.pdma(tap_d[name][:], src_sbuf, G)

            for dst, src in [(s_id, ident), (s_ones128, ones128), (s_ones1, ones1),
                             (s_blk, blk), (s_r8, r8), (s_convw, convw),
                             (s_convb, convb), (s_dtb, dtb), (s_dpar, dpar),
                             (s_dtln, dtln), (s_bcln, bcln)]:
                p.ydma(dst[:], src[:], L)
            for nm in ixnames:
                p.ydma(s_ix[nm][:], ixd[nm][:], L)
            p.ydma(s_aneg[:], alog[:], L)
            p.ydma(s_halfb[:], ffn_b[:], L)
            p.wait("vector", L)
            p.v("memset", s_cexp[:], float(1.0 / np.sqrt(DK)))
            p.v("tensor_scalar", out=s_halfb[:], in0=s_halfb[:], scalar1=0.5,
                scalar2=None, op0=ALU.mult)
            p.wait("scalar", L)
            p.a("activation", s_aneg[:], s_aneg[:], AF.Exp, sem=SA)
            p.drain("scalar", SA)
            p.wait("vector", SA)
            p.v("tensor_scalar", out=s_aneg[:], in0=s_aneg[:], scalar1=-1.0,
                scalar2=None, op0=ALU.mult, sem=SV)
            p.barrier()

            CH = NK * 128
            presrc = [("ffn", ffn_wt, DM, DM), ("wq", wq_t, DM, EDK),
                      ("wk", wk_t, DM, EDK), ("af", af_t, DM, ER),
                      ("bm2", bm2, ER, DM), ("ipx", ip_x, DM, DH),
                      ("ipz", ip_z, DM, DH), ("op", op_t, DH, DM)]
            chunks = []
            for nm, src, rows, cols in presrc:
                for r0 in range(0, rows, 128):
                    rr = min(128, rows - r0)
                    for c0 in range(0, cols, CH):
                        cc = min(CH, cols - c0)
                        chunks.append((nm, src, r0, rr, c0, cc))
            ps_sa, ps_sv, ps_g = [], [], []
            for ci, (nm, src, r0, rr, c0, cc) in enumerate(chunks):
                sl = (ci % 2) * CH
                bo = (ci % 2) * 2 * CH
                if ci >= 2:
                    p.wait("sync", SV, ps_sv[ci - 2])
                p.ydma(slab[0:rr, sl:sl + cc], src[r0:r0 + rr, c0:c0 + cc], Lp(ci))
                p.wait("scalar", Lp(ci))
                if ci >= 2:
                    p.wait("scalar", Gp(ci), ps_g[ci - 2])
                p.a("copy", big16[0:rr, bo:bo + cc], slab[0:rr, sl:sl + cc], sem=SA)
                ps_sa.append(SA.n)
                p.wait("vector", SA, ps_sa[ci])
                p.v("tensor_tensor", out=big16[0:rr, bo + CH:bo + CH + cc],
                    in0=slab[0:rr, sl:sl + cc], in1=big16[0:rr, bo:bo + cc],
                    op=ALU.subtract, sem=SV)
                ps_sv.append(SV.n)
                p.wait("gpsimd", SV, ps_sv[ci])
                p.pdma(w16[nm][0, r0:r0 + rr, c0:c0 + cc], big16[0:rr, bo:bo + cc],
                       Gp(ci))
                p.pdma(w16[nm][1, r0:r0 + rr, c0:c0 + cc],
                       big16[0:rr, bo + CH:bo + CH + cc], Gp(ci))
                ps_g.append(Gp(ci).n)
            p.waitall(G)
            p.waitall(G2)
            p.barrier()

            def ln_stats(src_tiles):
                rb_off, mrb_off = 5 * T, 6 * T
                m_off, v_off, t_off = 0, T, 2 * T
                ln_sa, ln_fr = [], []
                for k in range(NK):
                    xs_off = 3 * T if k % 2 == 0 else 7 * T
                    sq_off = 4 * T if k % 2 == 0 else 8 * T
                    if k >= 2:
                        p.wait("sync", SA, ln_sa[k - 2])
                        p.wait("sync", FR, ln_fr[k - 2])
                    p.ydma(W[:, xs_off:xs_off + T], src_tiles(k), Lp(k))
                    p.wait("scalar", Lp(k))
                    p.a("activation", W[:, sq_off:sq_off + T],
                        W[:, xs_off:xs_off + T], AF.Square, sem=SA)
                    ln_sa.append(SA.n)
                    p.wait("tensor", SA, ln_sa[k])
                    for w in range(NTW):
                        st, sp = (k == 0), (k == NK - 1)
                        p.t("matmul", pb[4 + w][0:1, 0:TW], s_ones128[:],
                            W[:, xs_off + w * TW:xs_off + (w + 1) * TW],
                            start=st, stop=sp)
                        p.t("matmul", pb[4 + NTW + w][0:1, 0:TW], s_ones128[:],
                            W[:, sq_off + w * TW:sq_off + (w + 1) * TW],
                            start=st, stop=sp,
                            sem=(FR if w == NTW - 1 else None))
                    ln_fr.append(FR.n)
                p.wait("vector", FR)
                for w in range(NTW):
                    p.v("tensor_scalar", out=W[0:1, m_off + w * TW:m_off + (w + 1) * TW],
                        in0=pb[4 + w][0:1, 0:TW], scalar1=1.0 / DM, scalar2=None,
                        op0=ALU.mult)
                    p.v("tensor_scalar", out=W[0:1, v_off + w * TW:v_off + (w + 1) * TW],
                        in0=pb[4 + NTW + w][0:1, 0:TW], scalar1=1.0 / DM, scalar2=None,
                        op0=ALU.mult)
                p.drain("vector")
                p.v("tensor_tensor", out=W[0:1, t_off:t_off + T],
                    in0=W[0:1, m_off:m_off + T], in1=W[0:1, m_off:m_off + T],
                    op=ALU.mult)
                p.drain("vector")
                p.v("tensor_tensor", out=W[0:1, v_off:v_off + T],
                    in0=W[0:1, v_off:v_off + T], in1=W[0:1, t_off:t_off + T],
                    op=ALU.subtract)
                p.drain("vector")
                p.v("tensor_scalar", out=W[0:1, v_off:v_off + T],
                    in0=W[0:1, v_off:v_off + T], scalar1=1e-5, scalar2=None,
                    op0=ALU.add, sem=SV)
                p.wait("scalar", SV)
                p.a("activation", W[0:1, v_off:v_off + T], W[0:1, v_off:v_off + T],
                    AF.Sqrt, sem=SA)
                p.wait("vector", SA)
                p.v("reciprocal", W[0:1, v_off:v_off + T], W[0:1, v_off:v_off + T])
                p.drain("vector")
                p.v("tensor_tensor", out=W[0:1, t_off:t_off + T],
                    in0=W[0:1, m_off:m_off + T], in1=W[0:1, v_off:v_off + T],
                    op=ALU.mult, sem=SV)
                p.wait("tensor", SV)
                for w in range(NTW):
                    p.t("matmul", pb[0][:, 0:TW], s_ones1[:],
                        W[0:1, v_off + w * TW:v_off + (w + 1) * TW],
                        start=True, stop=True)
                    p.t("matmul", pb[1][:, 0:TW], s_ones1[:],
                        W[0:1, t_off + w * TW:t_off + (w + 1) * TW],
                        start=True, stop=True, sem=ST)
                    p.wait("scalar", ST)
                    p.a("copy", W[:, rb_off + w * TW:rb_off + (w + 1) * TW], pb[0][:, 0:TW])
                    p.a("copy", W[:, mrb_off + w * TW:mrb_off + (w + 1) * TW],
                        pb[1][:, 0:TW], sem=SA)
                    p.wait("tensor", SA)
                return rb_off, mrb_off

            def xn_pass(src_tiles, rb_off, mrb_off, to_ag):
                xp_sv, xp_sp = [], []
                for k in range(NK):
                    xs_off = 3 * T if k % 2 == 0 else 7 * T
                    t1_off = 0 if k % 2 == 0 else T
                    xf_off = 2 * T if k % 2 == 0 else 4 * T
                    if k >= 2:
                        p.wait("sync", SV, xp_sv[k - 2][0])
                    else:
                        p.wait("sync", SA)
                        p.wait("sync", FR)
                        p.wait("sync", ST)
                    p.ydma(W[:, xs_off:xs_off + T], src_tiles(k), Lp(k))
                    p.wait("vector", Lp(k))
                    if k >= 2:
                        p.wait("vector", SP, xp_sp[k - 2][0])
                    else:
                        p.wait("vector", ST)
                    p.v("tensor_tensor", out=W[:, t1_off:t1_off + T],
                        in0=W[:, xs_off:xs_off + T], in1=W[:, rb_off:rb_off + T],
                        op=ALU.mult, sem=SV)
                    sv1 = SV.n
                    p.wait("gpsimd", SV, sv1)
                    if k >= 2:
                        p.wait("gpsimd", SV, xp_sv[k - 2][1])
                    p.p("tensor_tensor", out=W[:, xf_off:xf_off + T],
                        in0=W[:, t1_off:t1_off + T], in1=W[:, mrb_off:mrb_off + T],
                        op=ALU.subtract, sem=SP)
                    sp1 = SP.n
                    p.wait("vector", SP, sp1)
                    p.v("tensor_copy", xth(k), W[:, xf_off:xf_off + T], sem=SV)
                    sv2 = SV.n
                    p.wait("gpsimd", SV, sv2)
                    p.p("tensor_tensor", out=xtl(k), in0=W[:, xf_off:xf_off + T],
                        in1=xth(k), op=ALU.subtract, sem=SP)
                    sp2 = SP.n
                    xp_sv.append((sv1, sv2))
                    xp_sp.append((sp1, sp2))
                    if to_ag:
                        p.wait("gpsimd", SP, sp2)
                        p.pdma(agx_in[0, k, :, :], xth(k), G)
                        p.pdma(agx_in[1, k, :, :], xtl(k), G)

            qk_off = 0
            sc_off = NQC * T
            mx_off = sc_off + NTS * E
            w128_off = mx_off + NTS + 8
            h_off = w128_off + T
            eh_off = h_off + T
            t1e_off = 6 * T

            def ffn(src_tiles, dst_tiles):
                rb_off, mrb_off = ln_stats(src_tiles)
                xn_pass(src_tiles, rb_off, mrb_off, None)
                p.barrier()
                qk_st_h = []
                for k in range(NK):
                    sl = (k % 2) * 4 * EDK
                    if k >= 2:
                        p.wait("sync", ST, qk_st_h[k - 2])
                    p.ydma(slab16[:, sl:sl + EDK], w16["wq"][0, k * 128:(k + 1) * 128, :], Lp(k))
                    p.ydma(slab16[:, sl + EDK:sl + 2 * EDK],
                           w16["wq"][1, k * 128:(k + 1) * 128, :], Lp(k))
                    p.ydma(slab16[:, sl + 2 * EDK:sl + 3 * EDK],
                           w16["wk"][0, k * 128:(k + 1) * 128, :], Lp(k))
                    p.ydma(slab16[:, sl + 3 * EDK:sl + 4 * EDK],
                           w16["wk"][1, k * 128:(k + 1) * 128, :], Lp(k))
                    p.wait("tensor", Lp(k))
                    for tgt in range(2):
                        for pss in range(3):
                            wi = sl + (2 * tgt + (1 if pss == 1 else 0)) * EDK
                            xf = xth if pss < 2 else xtl
                            for qc in range(NQC):
                                for w in range(NTW):
                                    st_ = (k == 0 and pss == 0)
                                    sp_ = (k == NK - 1 and pss == 2)
                                    last = (tgt == 1 and pss == 2 and qc == NQC - 1
                                            and w == NTW - 1)
                                    p.t("matmul", pb[tgt * NQC * NTW + qc * NTW + w][:, 0:TW],
                                        slab16[:, wi + qc * 128:wi + (qc + 1) * 128],
                                        xf(k, w * TW, TW),
                                        start=st_, stop=sp_,
                                        sem=(ST if last else None))
                    qk_st_h.append(ST.n)
                p.wait("scalar", ST)
                p.wait("vector", ST)
                for qc in range(NQC):
                    for w in range(NTW):
                        p.a("copy", s_qsb[:, :], pb[qc * NTW + w][:, 0:TW], sem=SA)
                        p.wait("vector", SA)
                        p.v("tensor_tensor",
                            out=W[:, qk_off + qc * T + w * TW:
                                  qk_off + qc * T + (w + 1) * TW],
                            in0=s_qsb[:, :], in1=pb[NQC * NTW + qc * NTW + w][:, 0:TW],
                            op=ALU.mult, sem=SV)
                        p.wait("scalar", SV)
                p.wait("tensor", SV)
                for ts in range(NTS):
                    bank = ts % 2
                    if ts >= 2:
                        p.wait("tensor", SA)
                    for qc in range(NQC):
                        p.t("matmul", pb[bank][:, 0:E],
                            W[:, qk_off + qc * T + ts * 128:
                              qk_off + qc * T + (ts + 1) * 128],
                            s_blk[:, qc * E:(qc + 1) * E],
                            start=(qc == 0), stop=(qc == NQC - 1),
                            sem=(ST if qc == NQC - 1 else None))
                    p.wait("scalar", ST)
                    p.a("copy", W[:, sc_off + ts * E:sc_off + (ts + 1) * E],
                        pb[bank][:, 0:E], sem=SA)
                p.wait("vector", SA)
                sc3 = W[:, sc_off:sc_off + NTS * E].rearrange("p (s e) -> p s e", e=E)
                p.v("tensor_reduce", W[:, mx_off:mx_off + NTS], sc3, axis=AX.X,
                    op=ALU.max)
                p.drain("vector")
                mxb = W[:, mx_off:mx_off + NTS].rearrange(
                    "p s -> p s ()").broadcast_to((128, NTS, E))
                p.v("tensor_tensor", out=sc3, in0=sc3, in1=mxb, op=ALU.subtract)
                p.drain("vector", SV)
                p.wait("scalar", SV)
                p.a("activation", W[:, sc_off:sc_off + NTS * E],
                    W[:, sc_off:sc_off + NTS * E], AF.Exp, scale=s_cexp[:, 0:1],
                    sem=SA)
                p.wait("vector", SA)
                p.v("tensor_reduce", W[:, mx_off:mx_off + NTS], sc3, axis=AX.X,
                    op=ALU.add)
                p.drain("vector")
                p.v("reciprocal", W[:, mx_off:mx_off + NTS], W[:, mx_off:mx_off + NTS])
                p.drain("vector")
                p.v("tensor_tensor", out=sc3, in0=sc3, in1=mxb, op=ALU.mult, sem=SV)
                p.wait("tensor", SV)
                for ts in range(NTS):
                    bank = ts % 2
                    if ts >= 2:
                        p.wait("tensor", SA)
                    p.t("transpose", pb[bank][0:E, 0:128],
                        W[:, sc_off + ts * E:sc_off + (ts + 1) * E], s_id[:], sem=ST)
                    p.wait("scalar", ST)
                    p.a("copy", s_w8[0:E, ts * 128:(ts + 1) * 128],
                        pb[bank][0:E, 0:128], sem=SA)
                p.wait("tensor", SA)
                for w in range(NTW):
                    p.t("matmul", pb[2 + (w % 2)][:, 0:TW], s_r8[:],
                        s_w8[0:E, w * TW:(w + 1) * TW], start=True, stop=True, sem=ST)
                    p.wait("scalar", ST)
                    p.a("copy", W[:, w128_off + w * TW:w128_off + (w + 1) * TW],
                        pb[2 + (w % 2)][:, 0:TW], sem=SA)
                    p.wait("tensor", SA)
                p.wait("sync", SA)
                p.wait("sync", ST)
                p.ydma(slab16[:, 0:ER * NK].rearrange("p (k c) -> p k c", c=ER),
                       w16["af"][0, :, :].rearrange("(k p) c -> p k c", p=128), L)
                p.ydma(slab16[:, ER * NK:2 * ER * NK].rearrange(
                    "p (k c) -> p k c", c=ER),
                    w16["af"][1, :, :].rearrange("(k p) c -> p k c", p=128), L)
                p.wait("tensor", L)
                for w in range(NTW):
                    for pss in range(3):
                        wb = 0 if pss != 1 else ER * NK
                        xf = xth if pss < 2 else xtl
                        for k in range(NK):
                            p.t("matmul", pb[4 + (w % 2)][:, 0:TW],
                                slab16[:, wb + k * ER:wb + k * ER + 128],
                                xf(k, w * TW, TW),
                                start=(k == 0 and pss == 0),
                                stop=(k == NK - 1 and pss == 2),
                                sem=(ST if (k == NK - 1 and pss == 2) else None))
                    p.wait("scalar", ST)
                    p.a("copy", W[:, h_off + w * TW:h_off + (w + 1) * TW],
                        pb[4 + (w % 2)][:, 0:TW], sem=SA)
                p.wait("vector", SA)
                p.v("tensor_tensor", out=W[:, eh_off:eh_off + T],
                    in0=W[:, h_off:h_off + T], in1=W[:, w128_off:w128_off + T],
                    op=ALU.mult)
                p.drain("vector")
                eh16 = W[:, h_off:h_off + T].bitcast(BF16)
                p.v("tensor_copy", eh16[:, 0:T], W[:, eh_off:eh_off + T], sem=SV)
                p.wait("gpsimd", SV)
                p.p("tensor_tensor", out=eh16[:, T:2 * T],
                    in0=W[:, eh_off:eh_off + T], in1=eh16[:, 0:T],
                    op=ALU.subtract, sem=SP)
                p.barrier()
                st_h, sv_epi = [], []
                for o in range(NK):
                    sl = (o % 2) * 2 * NK * 128
                    bs = (o % 2) * 2
                    if o >= 2:
                        p.wait("sync", ST, st_h[o - 2])
                        p.wait("sync", SV, sv_epi[o - 2])
                    p.ydma(slab16[:, sl:sl + NK * 128].rearrange(
                        "p (k c) -> p k c", c=128),
                        w16["ffn"][0, :, o * 128:(o + 1) * 128].rearrange(
                            "(k p) c -> p k c", p=128), Lp(o))
                    p.ydma(slab16[:, sl + NK * 128:sl + 2 * NK * 128].rearrange(
                        "p (k c) -> p k c", c=128),
                        w16["ffn"][1, :, o * 128:(o + 1) * 128].rearrange(
                            "(k p) c -> p k c", p=128), Lp(o))
                    p.ydma(bm16[:, (o % 2) * 256:(o % 2) * 256 + 128],
                           w16["bm2"][0, :, o * 128:(o + 1) * 128], Lp(o))
                    p.ydma(bm16[:, (o % 2) * 256 + 128:(o % 2) * 256 + 256],
                           w16["bm2"][1, :, o * 128:(o + 1) * 128], Lp(o))
                    p.ydma(s_x0re[:, (o % 2) * T:(o % 2) * T + T], src_tiles(o), Lp(o))
                    p.wait("tensor", Lp(o))
                    if o >= 2:
                        p.wait("tensor", SV, sv_epi[o - 2])
                    for pss in range(3):
                        bwi = (o % 2) * 256 + (128 if pss == 1 else 0)
                        eo = 0 if pss < 2 else T
                        for w in range(NTW):
                            p.t("matmul", pb[bs + w][:, 0:TW],
                                bm16[:, bwi:bwi + 128],
                                eh16[:, eo + w * TW:eo + (w + 1) * TW],
                                start=(pss == 0), stop=False)
                    for k in range(NK):
                        for pss in range(3):
                            wb = sl + (NK * 128 if pss == 1 else 0)
                            xf = xth if pss < 2 else xtl
                            for w in range(NTW):
                                p.t("matmul", pb[bs + w][:, 0:TW],
                                    slab16[:, wb + k * 128:wb + (k + 1) * 128],
                                    xf(k, w * TW, TW),
                                    start=False, stop=(k == NK - 1 and pss == 2),
                                    sem=(ST if (k == NK - 1 and pss == 2
                                                and w == NTW - 1) else None))
                    st_h.append(ST.n)
                    p.wait("vector", ST, st_h[o])
                    if o >= 1:
                        p.wait("vector", G)
                    for w in range(NTW):
                        p.v("scalar_tensor_tensor",
                            out=W[:, t1e_off + w * TW:t1e_off + (w + 1) * TW],
                            in0=pb[bs + w][:, 0:TW], scalar=0.5,
                            in1=s_x0re[:, (o % 2) * T + w * TW:
                                       (o % 2) * T + (w + 1) * TW],
                            op0=ALU.mult, op1=ALU.add)
                    p.drain("vector")
                    p.v("tensor_scalar", out=W[:, t1e_off:t1e_off + T],
                        in0=W[:, t1e_off:t1e_off + T],
                        scalar1=s_halfb[:, o:o + 1], scalar2=None, op0=ALU.add,
                        sem=SV)
                    sv_epi.append(SV.n)
                    p.wait("gpsimd", SV, sv_epi[o])
                    p.pdma(dst_tiles(o), W[:, t1e_off:t1e_off + T], G)
                p.barrier()

            x0_t = lambda k: x0[k, :, :]
            x1_t = lambda k: x1d[k, :, :]
            x2_t = lambda k: x2d[k, :, :]
            out_t = lambda k: out[k, :, :]

            ffn(x0_t, x1_t)

            rb1, mrb1 = ln_stats(x1_t)
            xn_pass(x1_t, rb1, mrb1, lambda k: agx_in[k, :, :])
            p.wait("gpsimd", G)
            p.op("gpsimd", "collective_compute", "AllGather", ALU.bypass,
                 replica_groups=[list(range(NCORES))],
                 ins=[agx_in.rearrange("a k p t -> (a k p) t")],
                 outs=[agx_out.rearrange("r a k p t -> (r a k p) t")], sem=C)
            p.waitall(C)
            p.barrier()

            agxv = agx_out.rearrange("r a k p t -> (r a k p) t")
            xpad_off = 0
            xc_off = T + K - 1 + 61
            acc1_off = xc_off + T
            acc2_off = acc1_off + T
            for th in range(2):
                ixr = s_ix["ix_rhs0"] if th == 0 else s_ix["ix_rhs1"]
                for k in range(NK):
                    p.pgather(xth(k), agxv[:, :], ixr[:, 0:1], k * 128 * T, G)
                    p.pgather(xtl(k), agxv[:, :], ixr[:, 0:1],
                              (NK + k) * 128 * T, G)
                p.waitall(G)
                ip_st_h, ip_sa_h = [], []
                pend_ssm = None

                def ssm_block(scb, sa_val):
                    p.wait("tensor", SA, sa_val)
                    ngrp = (NTS + 3) // 4
                    for g in range(ngrp):
                        t0g, t1g = g * 4, min((g + 1) * 4, NTS)
                        if g >= 1:
                            p.wait("tensor", SV)
                        for ts in range(t0g, t1g):
                            p.t("matmul", pb[4 + ts % 4][:, 0:RQ],
                                W[:, xc_off + ts * 128:xc_off + (ts + 1) * 128],
                                s_xq[:, (scb % 2) * RQ:(scb % 2) * RQ + RQ],
                                start=True, stop=True,
                                sem=(FR if ts == t1g - 1 else None))
                        p.wait("vector", FR)
                        for ts in range(t0g, t1g):
                            so = (th * NTS + ts) * RQ
                            if scb == 0:
                                p.v("tensor_copy", s_ssm[:, so:so + RQ],
                                    pb[4 + ts % 4][:, 0:RQ],
                                    sem=(SV if ts == t1g - 1 else None))
                            else:
                                p.v("tensor_tensor", out=s_ssm[:, so:so + RQ],
                                    in0=s_ssm[:, so:so + RQ],
                                    in1=pb[4 + ts % 4][:, 0:RQ],
                                    op=ALU.add, sem=(SV if ts == t1g - 1 else None))

                for cb in range(2 * NDT):
                    isx = cb < NDT
                    di = cb if isx else cb - NDT
                    sl = (cb % 2) * 2 * NK * 128
                    bs = (cb % 2) * 2
                    w16src = w16["ipx"] if isx else w16["ipz"]
                    if cb >= 2:
                        p.wait("sync", ST, ip_st_h[cb - 2][0])
                        p.wait("sync", FR, ip_st_h[cb - 1][1])
                    p.ydma(slab16[:, sl:sl + NK * 128].rearrange(
                        "p (k c) -> p k c", c=128),
                        w16src[0, :, di * 128:(di + 1) * 128].rearrange(
                            "(k p) c -> p k c", p=128), Lp(cb))
                    p.ydma(slab16[:, sl + NK * 128:sl + 2 * NK * 128].rearrange(
                        "p (k c) -> p k c", c=128),
                        w16src[1, :, di * 128:(di + 1) * 128].rearrange(
                            "(k p) c -> p k c", p=128), Lp(cb))
                    if isx:
                        p.ydma(s_xq[:, (cb % 2) * RQ:(cb % 2) * RQ + RQ],
                               xp_t[di * 128:(di + 1) * 128, :], Lp(cb))
                    p.wait("tensor", Lp(cb))
                    if cb >= 2:
                        p.wait("tensor", SA, ip_sa_h[cb - 2])
                    for k in range(NK):
                        for pss in range(3):
                            wb = sl + (NK * 128 if pss == 1 else 0)
                            xf = xth if pss < 2 else xtl
                            for w in range(NTW):
                                p.t("matmul", pb[bs + w][:, 0:TW],
                                    slab16[:, wb + k * 128:wb + (k + 1) * 128],
                                    xf(k, w * TW, TW),
                                    start=(k == 0 and pss == 0),
                                    stop=(k == NK - 1 and pss == 2),
                                    sem=(ST if (k == NK - 1 and pss == 2
                                                and w == NTW - 1) else None))
                    if pend_ssm is not None:
                        ssm_block(*pend_ssm)
                        pend_ssm = None
                    if isx:
                        p.wait("scalar", ST)
                        p.wait("scalar", SP)
                        for w in range(NTW):
                            p.a("copy", W[:, xpad_off + (K - 1) + w * TW:
                                          xpad_off + (K - 1) + (w + 1) * TW],
                                pb[bs + w][:, 0:TW], sem=(SA if w == NTW - 1 else None))
                        ip_sa_h.append(SA.n)
                        p.wait("vector", SA)
                        if th == 0:
                            p.v("memset", W[:, xpad_off:xpad_off + (K - 1)], 0.0)
                        else:
                            p.v("tensor_copy", W[:, xpad_off:xpad_off + (K - 1)],
                                s_halo[:, di * (K - 1):(di + 1) * (K - 1)])
                        p.drain("vector")
                        if th == 0:
                            p.v("tensor_copy",
                                s_halo[:, di * (K - 1):(di + 1) * (K - 1)],
                                W[:, xpad_off + T:xpad_off + T + (K - 1)])
                            p.drain("vector")
                        p.v("tensor_scalar", out=W[:, acc1_off:acc1_off + T],
                            in0=W[:, xpad_off:xpad_off + T],
                            scalar1=s_convw[:, di * K:di * K + 1],
                            scalar2=s_convb[:, di:di + 1],
                            op0=ALU.mult, op1=ALU.add)
                        p.drain("vector")
                        p.v("scalar_tensor_tensor", out=W[:, acc2_off:acc2_off + T],
                            in0=W[:, xpad_off + 1:xpad_off + 1 + T],
                            scalar=s_convw[:, di * K + 1:di * K + 2],
                            in1=W[:, acc1_off:acc1_off + T],
                            op0=ALU.mult, op1=ALU.add)
                        p.drain("vector")
                        p.v("scalar_tensor_tensor", out=W[:, acc1_off:acc1_off + T],
                            in0=W[:, xpad_off + 2:xpad_off + 2 + T],
                            scalar=s_convw[:, di * K + 2:di * K + 3],
                            in1=W[:, acc2_off:acc2_off + T],
                            op0=ALU.mult, op1=ALU.add)
                        p.drain("vector")
                        p.v("scalar_tensor_tensor", out=W[:, acc2_off:acc2_off + T],
                            in0=W[:, xpad_off + 3:xpad_off + 3 + T],
                            scalar=s_convw[:, di * K + 3:di * K + 4],
                            in1=W[:, acc1_off:acc1_off + T],
                            op0=ALU.mult, op1=ALU.add, sem=SV)
                        p.drain("vector", SP)
                        p.wait("scalar", SP)
                        p.wait("scalar", GS)
                        p.wait("scalar", GS2)
                        p.wait("scalar", FR)
                        p.a("activation", W[:, xc_off:xc_off + T],
                            W[:, acc2_off:acc2_off + T], AF.Silu, sem=SA)
                        p.wait("gpsimd", SA)
                        p.pdma(xcd[th, di, :, :], W[:, xc_off:xc_off + T], GSp(cb))
                        pend_ssm = (cb, SA.n)
                    else:
                        p.wait("scalar", ST)
                        p.wait("scalar", SP)
                        p.wait("scalar", GS)
                        p.wait("scalar", GS2)
                        for w in range(NTW):
                            p.a("activation",
                                W[:, acc1_off + w * TW:acc1_off + (w + 1) * TW],
                                pb[bs + w][:, 0:TW], AF.Silu,
                                sem=(SA if w == NTW - 1 else None))
                        ip_sa_h.append(SA.n)
                        p.wait("gpsimd", SA)
                        p.pdma(szd[th, di, :, :], W[:, acc1_off:acc1_off + T],
                               GSp(cb))
                    ip_st_h.append((ST.n, FR.n))
                p.barrier()
            p.wait("gpsimd", SV)
            p.pdma(agp_in.rearrange("s p q -> p s q"),
                   s_ssm[:, :].rearrange("p (s q) -> p s q", q=RQ), G)
            p.waitall(G)
            p.op("gpsimd", "collective_compute", "AllGather", ALU.bypass,
                 replica_groups=[list(range(NCORES))],
                 ins=[agp_in.rearrange("s p q -> (s p) q")],
                 outs=[agp_out.rearrange("r s p q -> (r s p) q")], sem=C)
            p.waitall(C)
            agpv = agp_out.rearrange("r s p q -> (r s p) q")
            for half in range(2):
                ix = s_ix["ix_pa"] if half == 0 else s_ix["ix_pb"]
                for ts2 in range(2 * NTS):
                    go = 0 if half == 0 else RQ
                    p.pgather(s_gt[:, go:go + RQ], agpv[:, :], ix[:, 0:1],
                              ts2 * 128 * RQ, G)
                    p.wait("vector", G)
                    if half == 0:
                        p.v("tensor_copy", s_ssm[:, ts2 * RQ:(ts2 + 1) * RQ],
                            s_gt[:, go:go + RQ], sem=SV)
                    else:
                        p.v("tensor_tensor", out=s_ssm[:, ts2 * RQ:(ts2 + 1) * RQ],
                            in0=s_ssm[:, ts2 * RQ:(ts2 + 1) * RQ],
                            in1=s_gt[:, go:go + RQ], op=ALU.add, sem=SV)
                    p.wait("gpsimd", SV)
            p.barrier()
            tap("ssm", s_ssm[:, :])

            for ts2 in range(2 * NTS):
                so = ts2 * RQ
                p.wait("scalar", SA)
                p.a("activation", W[:, 0:R], s_ssm[:, so:so + R], AF.Square, sem=SA)
                p.wait("vector", SA)
                p.v("tensor_reduce", W[:, R:R + 1], W[:, 0:R], axis=AX.X, op=ALU.add)
                p.drain("vector")
                p.v("tensor_scalar", out=W[:, R:R + 1], in0=W[:, R:R + 1],
                    scalar1=1.0 / R, scalar2=1e-6, op0=ALU.mult, op1=ALU.add, sem=SV)
                p.wait("scalar", SV)
                p.a("activation", W[:, R:R + 1], W[:, R:R + 1], AF.Sqrt, sem=SA)
                p.wait("vector", SA)
                p.v("reciprocal", W[:, R:R + 1], W[:, R:R + 1])
                p.drain("vector")
                p.v("tensor_scalar", out=W[:, 0:R], in0=s_ssm[:, so:so + R],
                    scalar1=W[:, R:R + 1], scalar2=None, op0=ALU.mult, sem=SV)
                p.wait("gpsimd", SV)
                p.p("tensor_tensor", out=W[:, 0:R], in0=W[:, 0:R], in1=s_dtln[:, :],
                    op=ALU.mult, sem=SP)
                for bi in range(2):
                    qo = so + R + bi * N
                    wo = 2 * R
                    p.wait("scalar", SP if bi == 0 else SP)
                    p.a("activation", W[:, wo:wo + N], s_ssm[:, qo:qo + N],
                        AF.Square, sem=SA)
                    p.wait("vector", SA)
                    p.v("tensor_reduce", W[:, R:R + 1], W[:, wo:wo + N],
                        axis=AX.X, op=ALU.add)
                    p.drain("vector")
                    p.v("tensor_scalar", out=W[:, R:R + 1], in0=W[:, R:R + 1],
                        scalar1=1.0 / N, scalar2=1e-6, op0=ALU.mult, op1=ALU.add,
                        sem=SV)
                    p.wait("scalar", SV)
                    p.a("activation", W[:, R:R + 1], W[:, R:R + 1], AF.Sqrt, sem=SA)
                    p.wait("vector", SA)
                    p.v("reciprocal", W[:, R:R + 1], W[:, R:R + 1])
                    p.drain("vector")
                    p.v("tensor_scalar", out=W[:, wo:wo + N],
                        in0=s_ssm[:, qo:qo + N], scalar1=W[:, R:R + 1], scalar2=None,
                        op0=ALU.mult, sem=SV)
                    p.wait("gpsimd", SV)
                    p.p("tensor_tensor",
                        out=W[:, 3 * R + bi * N:3 * R + (bi + 1) * N],
                        in0=W[:, wo:wo + N],
                        in1=s_bcln[:, bi * N:(bi + 1) * N], op=ALU.mult, sem=SP)
                p.wait("tensor", SP)
                for rc in range(NRC):
                    p.t("transpose", pb[6 + rc % 2][:, 0:128],
                        W[:, rc * 128:(rc + 1) * 128], s_id[:], sem=ST)
                    p.wait("scalar", ST)
                    p.a("copy", big[:, SCAN_DTRT + rc * SEQ + ts2 * 128:
                                  SCAN_DTRT + rc * SEQ + (ts2 + 1) * 128],
                        pb[6 + rc % 2][:, 0:128], sem=SA)
                    p.wait("tensor", SA)
                p.t("transpose", pb[6][0:BCW, 0:128], W[:, 3 * R:3 * R + BCW],
                    s_id[:], sem=ST)
                p.wait("scalar", ST)
                p.a("copy", s_bc[:, ts2 * 128:(ts2 + 1) * 128],
                    pb[6][0:BCW, 0:128], sem=SA)
                p.wait("tensor", SA)
                p.wait("vector", SA)
            p.wait("gpsimd", SA)
            p.pdma(bcd[:, :], s_bc[:, :], G)
            p.waitall(G)
            p.waitall(L)
            p.barrier()

            zh_off = NRC * SEQ
            scr = zh_off + N * TC
            zu_off, zdt_off, ze1_off = scr, scr + TC, scr + 2 * TC
            zya_off, yzy_off = scr + 3 * TC, scr + 4 * TC
            zxcs_off = scr + 5 * TC
            zszs_off = scr + 7 * TC
            zdtp_off = scr + 9 * TC
            zbb_off = scr + 9 * TC + 512
            zcb_off = zbb_off + N * TC
            zda_off = zcb_off + N * TC
            zdb_off = zda_off + N * TC
            assert zdb_off + N * TC <= BIGW
            sbb2 = big[:, zbb_off:zbb_off + N * TC]
            scb2 = big[:, zcb_off:zcb_off + N * TC]
            u2_off = zu_off
            p.v("memset", s_hst[:, :], 0.0)
            p.drain("vector", SV)
            NIT = NTC * NDT
            NYV = 4
            h_st = [None] * (NIT + 1)
            h_ln = [None] * (NIT + 1)
            h_da = [None] * NIT
            h_u = [None] * (NIT + 1)
            h_scan = [None] * NIT
            h_y = [None] * NIT
            h_db = [None] * (NIT + 1)
            h_ymp = [None] * NIT

            h_ymv = [None] * NIT

            def sc_bbcb(i2):
                tc2 = i2 // NDT
                if i2 >= 1:
                    p.wait("sync", SP, h_db[i2 - 1])
                    p.wait("sync", SP, h_ymp[i2 - 1])
                    p.wait("sync", SV, h_ymv[i2 - 1])
                for n in range(N):
                    p.ydma(sbb2[:, n * TC:(n + 1) * TC],
                           bcd[n:n + 1, tc2 * TC:(tc2 + 1) * TC]
                           .partition_broadcast(128).squeeze(1), LB)
                    p.ydma(scb2[:, n * TC:(n + 1) * TC],
                           bcd[N + n:N + n + 1, tc2 * TC:(tc2 + 1) * TC]
                           .partition_broadcast(128).squeeze(1), LB)

            def sc_loads(i2):
                tc2, di2 = i2 // NDT, i2 % NDT
                th2, t02 = tc2 // NTC_H, (tc2 % NTC_H) * TC
                par = i2 % 2
                if i2 >= 2:
                    p.wait("sync", SV, h_y[i2 - 2])
                    p.wait("sync", ST, h_st[i2 - 2])
                p.ydma(big[:, zxcs_off + par * TC:zxcs_off + (par + 1) * TC],
                       xcd[th2, di2, :, t02:t02 + TC], Lp(i2))
                p.ydma(big[:, zszs_off + par * TC:zszs_off + (par + 1) * TC],
                       szd[th2, di2, :, t02:t02 + TC], Lp(i2))
                p.ydma(big[:, zdtp_off + par * 256:zdtp_off + par * 256 + 256]
                       .rearrange("p (rc c) -> p rc c", c=128),
                       dtp_t[:, di2 * 128:(di2 + 1) * 128].rearrange(
                           "(rc p) c -> p rc c", p=128), Lp(i2))

            def sc_dtmm(i2):
                tc2, di2 = i2 // NDT, i2 % NDT
                par = i2 % 2
                p.wait("tensor", Lp(i2))
                if i2 >= 2:
                    p.wait("tensor", SA, h_ln[i2 - 1])
                for rc in range(NRC):
                    p.t("matmul", pb[par][:, 0:TC],
                        big[:, zdtp_off + par * 256 + rc * 128:
                            zdtp_off + par * 256 + (rc + 1) * 128],
                        big[:, rc * SEQ + tc2 * TC:rc * SEQ + (tc2 + 1) * TC],
                        start=(rc == 0), stop=(rc == NRC - 1),
                        sem=(ST if rc == NRC - 1 else None))
                h_st[i2] = ST.n

            def sc_softplus(i2):
                di2 = i2 % NDT
                par = i2 % 2
                p.wait("scalar", ST, h_st[i2])
                if i2 >= 1:
                    p.wait("scalar", SV, h_u[i2 - 1])
                p.a("activation", big[:, ze1_off:ze1_off + TC], pb[par][:, 0:TC],
                    AF.Exp, bias=s_dtb[:, di2:di2 + 1])
                p.drain("scalar")
                p.a("activation", big[:, zdt_off:zdt_off + TC],
                    big[:, ze1_off:ze1_off + TC], AF.Ln, bias=1.0, sem=SA)
                h_ln[i2] = SA.n
                p.drain("scalar")

            def sc_u(i2):
                par = i2 % 2
                p.wait("vector", SA, h_ln[i2])
                p.wait("vector", Lp(i2))
                p.v("tensor_tensor", out=big[:, zu_off:zu_off + TC],
                    in0=big[:, zdt_off:zdt_off + TC],
                    in1=big[:, zxcs_off + par * TC:zxcs_off + (par + 1) * TC],
                    op=ALU.mult, sem=SV)
                h_u[i2] = SV.n

            def sc_dbx(i2):
                p.wait("gpsimd", LB)
                p.wait("gpsimd", SV, h_u[i2])
                if i2 >= 1:
                    p.wait("gpsimd", SV, h_scan[i2 - 1])
                for n in range(N):
                    p.p("tensor_tensor",
                        out=big[:, zdb_off + n * TC:zdb_off + (n + 1) * TC],
                        in0=big[:, zu_off:zu_off + TC],
                        in1=sbb2[:, n * TC:(n + 1) * TC], op=ALU.mult,
                        sem=(SP if n == N - 1 else None))
                h_db[i2] = SP.n

            sc_bbcb(0)
            sc_loads(0)
            sc_dtmm(0)
            sc_softplus(0)
            sc_u(0)
            sc_dbx(0)
            for idx in range(NIT):
                tc, dt_i = idx // NDT, idx % NDT
                par = idx % 2
                nxt = idx + 1
                if nxt < NIT:
                    sc_loads(nxt)
                    sc_dtmm(nxt)
                if idx >= 1:
                    p.wait("scalar", SV, h_scan[idx - 1])
                for n in range(N):
                    p.a("activation",
                        big[:, zda_off + n * TC:zda_off + (n + 1) * TC],
                        big[:, zdt_off:zdt_off + TC], AF.Exp,
                        scale=s_aneg[:, dt_i * N + n:dt_i * N + n + 1],
                        sem=(SA if n == N - 1 else None))
                h_da[idx] = SA.n
                if nxt < NIT:
                    sc_softplus(nxt)
                p.wait("vector", SA, h_da[idx])
                p.wait("vector", SP, h_db[idx])
                for n in range(N):
                    p.v("tensor_tensor_scan",
                        big[:, zh_off + n * TC:zh_off + (n + 1) * TC],
                        big[:, zda_off + n * TC:zda_off + (n + 1) * TC],
                        big[:, zdb_off + n * TC:zdb_off + (n + 1) * TC],
                        s_hst[:, dt_i * N + n:dt_i * N + n + 1],
                        ALU.mult, ALU.add)
                p.drain("vector")
                hv = big[:, zh_off:zh_off + N * TC].rearrange(
                    "p (n t) -> p n t", t=TC)
                p.v("tensor_copy", s_hst[:, dt_i * N:(dt_i + 1) * N],
                    hv[:, :, TC - 1])
                p.drain("vector", SV)
                h_scan[idx] = SV.n
                p.wait("vector", LB)
                for n in range(NYV):
                    p.v("tensor_tensor",
                        out=big[:, zh_off + n * TC:zh_off + (n + 1) * TC],
                        in0=big[:, zh_off + n * TC:zh_off + (n + 1) * TC],
                        in1=scb2[:, n * TC:(n + 1) * TC], op=ALU.mult)
                p.drain("vector", SV)
                h_ymv[idx] = SV.n
                p.wait("gpsimd", SV, h_scan[idx])
                p.wait("gpsimd", LB)
                for n in range(NYV, N):
                    p.p("tensor_tensor",
                        out=big[:, zh_off + n * TC:zh_off + (n + 1) * TC],
                        in0=big[:, zh_off + n * TC:zh_off + (n + 1) * TC],
                        in1=scb2[:, n * TC:(n + 1) * TC], op=ALU.mult,
                        sem=(SP if n == N - 1 else None))
                h_ymp[idx] = SP.n
                if nxt < NIT:
                    if nxt % NDT == 0:
                        sc_bbcb(nxt)
                    sc_u(nxt)
                    sc_dbx(nxt)
                p.wait("vector", SP, h_ymp[idx])
                hv2 = big[:, zh_off:zh_off + N * TC].rearrange(
                    "p (n t) -> p t n", t=TC)
                p.v("tensor_reduce", big[:, zya_off:zya_off + TC], hv2, axis=AX.X,
                    op=ALU.add)
                p.drain("vector")
                p.wait("vector", GS)
                p.v("scalar_tensor_tensor", out=big[:, yzy_off:yzy_off + TC],
                    in0=big[:, zxcs_off + par * TC:zxcs_off + (par + 1) * TC],
                    scalar=s_dpar[:, dt_i:dt_i + 1],
                    in1=big[:, zya_off:zya_off + TC], op0=ALU.mult, op1=ALU.add)
                p.drain("vector")
                p.v("tensor_tensor", out=big[:, yzy_off:yzy_off + TC],
                    in0=big[:, yzy_off:yzy_off + TC],
                    in1=big[:, zszs_off + par * TC:zszs_off + (par + 1) * TC],
                    op=ALU.mult, sem=SV)
                h_y[idx] = SV.n
                p.wait("gpsimd", SV, h_y[idx])
                p.pdma(yd[dt_i, :, tc * TC:(tc + 1) * TC],
                       big[:, yzy_off:yzy_off + TC], GS)
            p.waitall(GS)
            p.barrier()

            for th in range(2):
                yl_sp = []
                for k in range(NDT):
                    yso = 6 * T + (k % 2) * T
                    if k >= 2:
                        p.wait("sync", SP, yl_sp[k - 2])
                    p.ydma(W[:, yso:yso + T], yd[k, :, th * T:th * T + T], Lp(k))
                    p.wait("scalar", Lp(k))
                    p.a("copy", xth(k), W[:, yso:yso + T], sem=SA)
                    p.wait("gpsimd", SA)
                    p.p("tensor_tensor", out=xtl(k), in0=W[:, yso:yso + T],
                        in1=xth(k), op=ALU.subtract, sem=SP)
                    yl_sp.append(SP.n)
                p.waitall(SP)
                op_st_h, op_sa_h = [], []
                for dm in range(NK):
                    sl = (dm % 2) * 2 * NK * 128
                    bs = (dm % 2) * 2
                    if dm >= 2:
                        p.wait("sync", ST, op_st_h[dm - 2])
                    p.ydma(slab16[:, sl:sl + NDT * 128].rearrange(
                        "p (k c) -> p k c", c=128),
                        w16["op"][0, :, dm * 128:(dm + 1) * 128].rearrange(
                            "(k p) c -> p k c", p=128), Lp(dm))
                    p.ydma(slab16[:, sl + NDT * 128:sl + 2 * NDT * 128].rearrange(
                        "p (k c) -> p k c", c=128),
                        w16["op"][1, :, dm * 128:(dm + 1) * 128].rearrange(
                            "(k p) c -> p k c", p=128), Lp(dm))
                    p.wait("tensor", Lp(dm))
                    if dm >= 2:
                        p.wait("tensor", SA, op_sa_h[dm - 2])
                    for k in range(NDT):
                        for pss in range(3):
                            wb = sl + (NDT * 128 if pss == 1 else 0)
                            xf = xth if pss < 2 else xtl
                            for w in range(NTW):
                                p.t("matmul", pb[bs + w][:, 0:TW],
                                    slab16[:, wb + k * 128:wb + (k + 1) * 128],
                                    xf(k, w * TW, TW),
                                    start=(k == 0 and pss == 0),
                                    stop=(k == NDT - 1 and pss == 2),
                                    sem=(ST if (k == NDT - 1 and pss == 2
                                                and w == NTW - 1) else None))
                    op_st_h.append(ST.n)
                    p.wait("scalar", ST, op_st_h[dm])
                    p.wait("scalar", G)
                    for w in range(NTW):
                        p.a("copy", W[:, w * TW:(w + 1) * TW], pb[bs + w][:, 0:TW],
                            sem=(SA if w == NTW - 1 else None))
                    op_sa_h.append(SA.n)
                    p.wait("gpsimd", SA, op_sa_h[dm])
                    p.pdma(pard[th, dm, :, :], W[:, 0:T], G)
                p.barrier()

            parv = pard.rearrange("a k p t -> (a k p) t")
            p.wait("gpsimd", G)
            p.wait("gpsimd", G2)
            gg_h = []
            for k in range(NK):
                if k >= 2:
                    p.wait("gpsimd", GSp(k), gg_h[k - 2])
                p.pgather(s_gt[:, (k % 2) * T:(k % 2) * T + T], parv[:, :],
                          s_ix["ix_gsrc"][:, 0:1], k * 128 * T, Gp(k))
                p.wait("gpsimd", Gp(k))
                p.pdma(agg_in[k, :, :], s_gt[:, (k % 2) * T:(k % 2) * T + T], GSp(k))
                gg_h.append(GSp(k).n)
            p.waitall(GS)
            p.waitall(GS2)
            p.op("gpsimd", "collective_compute", "AllGather", ALU.bypass,
                 replica_groups=[list(range(NCORES))],
                 ins=[agg_in.rearrange("k p t -> (k p) t")],
                 outs=[agg_out.rearrange("r k p t -> (r k p) t")], sem=C)
            p.waitall(C)
            p.barrier()

            aggv = agg_out.rearrange("r k p t -> (r k p) t")
            for k in range(NK):
                so = (k % 2) * T
                if k >= 2:
                    p.wait("sync", SP)
                    p.wait("sync", GS)
                    p.wait("gpsimd", SP)
                p.ydma(s_xa[:, so:so + T], x1d[k, :, :], Lp(k))
                p.pgather(s_xb[:, so:so + T], parv[:, :],
                          s_ix["ix_own"][:, 0:1], k * 128 * T, Gp(k))
                p.pgather(s_xc2[:, so:so + T], aggv[:, :],
                          s_ix["ix_peer"][:, 0:1], k * 128 * T, Gp(k))
                p.wait("vector", Lp(k))
                p.wait("vector", Gp(k))
                p.v("tensor_tensor", out=s_xa[:, so:so + T],
                    in0=s_xa[:, so:so + T], in1=s_xb[:, so:so + T],
                    op=ALU.add, sem=SV)
                p.wait("gpsimd", SV)
                p.p("tensor_tensor", out=s_xa[:, so:so + T],
                    in0=s_xa[:, so:so + T], in1=s_xc2[:, so:so + T],
                    op=ALU.add, sem=SP)
                p.wait("gpsimd", SP)
                p.pdma(x2d[k, :, :], s_xa[:, so:so + T], GS)
            p.waitall(GS)
            p.barrier()

            ffn(x2_t, out_t)
            p.waitall(G)

        @block.sync
        def _(eng):
            walk("sync", eng)

        @block.vector
        def _(eng):
            walk("vector", eng)

        @block.scalar
        def _(eng):
            walk("scalar", eng)

        @block.tensor
        def _(eng):
            walk("tensor", eng)

        @block.gpsimd
        def _(eng):
            walk("gpsimd", eng)

    return nc



def make_inputs(C, ins, core):
    c = C
    b, h = core // 2, core % 2
    T, DH, NK, NDT = c["T"], c["DH"], c["NK"], c["NDT"]
    DM, SEQ, R, N, K, E, DK, ER = (c["DM"], c["SEQ"], c["R"], c["N"], c["K"],
                                   c["E"], c["DK"], c["ER"])
    NQC = c["NQC"]
    dlo = h * DH
    f32 = lambda a: np.ascontiguousarray(a, dtype=np.float32)
    m = {}
    hs = ins["L0_attn_hidden_states"]
    m["x0"] = f32(hs[b, h * T:(h + 1) * T, :].T.reshape(NK, 128, T))
    m["ffn_wt"] = f32(ins["L0_ffn_W"].T)
    m["ffn_b"] = f32(ins["L0_ffn_b"].reshape(NK, 128).T)
    m["wq_t"] = f32(ins["L0_ffn_Wq"].T)
    m["wk_t"] = f32(ins["L0_ffn_Wk"].T)
    m["af_t"] = f32(ins["L0_ffn_A"].transpose(2, 0, 1).reshape(DM, ER))
    m["bm2"] = f32(ins["L0_ffn_Bm"].transpose(0, 2, 1).reshape(ER, DM))
    m["ip_x"] = f32(ins["L0_attn_in_proj_w"][:, dlo:dlo + DH])
    m["ip_z"] = f32(ins["L0_attn_in_proj_w"][:, c["DIN"] + dlo:c["DIN"] + dlo + DH])
    m["convw"] = f32(ins["L0_attn_conv_w"][dlo:dlo + DH].reshape(NDT, 128, K)
                     .transpose(1, 0, 2).reshape(128, NDT * K))
    m["convb"] = f32(ins["L0_attn_conv_b"][dlo:dlo + DH].reshape(NDT, 128).T)
    m["xp_t"] = f32(ins["L0_attn_x_proj_w"][:, dlo:dlo + DH].T)
    m["dtp_t"] = f32(ins["L0_attn_dt_proj_w"][dlo:dlo + DH, :].T)
    m["dtb"] = f32(ins["L0_attn_dt_bias"][dlo:dlo + DH].reshape(NDT, 128).T)
    m["alog"] = f32(ins["L0_attn_A_log"][dlo:dlo + DH].reshape(NDT, 128, N)
                    .transpose(1, 0, 2).reshape(128, NDT * N))
    m["dpar"] = f32(ins["L0_attn_D_param"][dlo:dlo + DH].reshape(NDT, 128).T)
    m["op_t"] = f32(ins["L0_attn_out_proj_w"][:, dlo:dlo + DH].T)
    m["dtln"] = f32(np.broadcast_to(ins["L0_attn_dt_ln_w"][None, :], (128, R)))
    m["bcln"] = f32(np.broadcast_to(
        np.concatenate([np.asarray(ins["L0_attn_B_ln_w"]),
                        np.asarray(ins["L0_attn_C_ln_w"])])[None, :], (128, 2 * N)))
    m["ident"] = np.eye(128, dtype=np.float32)
    m["ones128"] = np.ones((128, 1), np.float32)
    m["ones1"] = np.ones((1, 128), np.float32)
    blkm = np.zeros((128, NQC * E), np.float32)
    for kt in range(NQC):
        for pp in range(128):
            n_ = kt * 128 + pp
            blkm[pp, kt * E + n_ // DK] = 1.0
    m["blk"] = blkm
    r8m = np.zeros((E, 128), np.float32)
    rr = ER // E
    for e in range(E):
        r8m[e, e * rr:(e + 1) * rr] = 1.0
    m["r8"] = r8m
    ar = np.arange(128, dtype=np.uint32)[:, None]
    u = np.uint32
    m["ix_rhs0"] = u((2 * b) * 2 * NK * 128) + ar
    m["ix_rhs1"] = u((2 * b + 1) * 2 * NK * 128) + ar
    m["ix_pa"] = u((2 * b) * (2 * c["NTS"]) * 128) + ar
    m["ix_pb"] = u((2 * b + 1) * (2 * c["NTS"]) * 128) + ar
    m["ix_own"] = u(h * NK * 128) + ar
    m["ix_gsrc"] = u((1 - h) * NK * 128) + ar
    m["ix_peer"] = u((core ^ 1) * NK * 128) + ar
    for kk in ["ix_rhs0", "ix_rhs1", "ix_pa", "ix_pb", "ix_own", "ix_gsrc",
               "ix_peer"]:
        m[kk] = np.ascontiguousarray(m[kk], dtype=np.uint32)
    return m


def assemble(C, results):
    T = C["T"]
    full = np.empty((len(results) // 2, C["SEQ"], C["DM"]), np.float32)
    for core in range(len(results)):
        b, h = core // 2, core % 2
        o = results[core]["out"].reshape(C["DM"], T)
        full[b, h * T:(h + 1) * T, :] = o.T
    return full


def kernel(**inputs):
    C = derive(cfg_full())
    nc = build_program(C)
    in_maps = [make_inputs(C, inputs, core) for core in range(NCORES)]
    res = run_bass_kernel_spmd(nc, in_maps, core_ids=list(range(NCORES)))
    return assemble(C, res.results)


# version 12 (final)
# speedup vs baseline: 1.0366x; 1.0366x over previous
import numpy as np
import concourse.bass as bass
import concourse.mybir as mybir
from concourse.bass_utils import run_bass_kernel_spmd
from concourse.bass_core_rust import IndirectOffsetOnAxis
from contextlib import ExitStack

F32 = mybir.dt.float32
BF16 = mybir.dt.bfloat16
U32 = mybir.dt.uint32
AF = mybir.ActivationFunctionType
ALU = mybir.AluOpType
AX = mybir.AxisListType

NCORES = 8


def cfg_full():
    return dict(DM=4096, SEQ=2048, DIN=8192, R=256, N=16, K=4, E=8, DK=32, ER=128)


def cfg_small():
    return dict(DM=512, SEQ=256, DIN=1024, R=256, N=16, K=4, E=8, DK=32, ER=128)


def derive(c):
    d = dict(c)
    d["T"] = c["SEQ"] // 2
    d["DH"] = c["DIN"] // 2
    d["NK"] = c["DM"] // 128
    d["NDT"] = d["DH"] // 128
    d["NTS"] = d["T"] // 128
    d["TW"] = min(512, d["T"])
    d["NTW"] = d["T"] // d["TW"]
    d["TC"] = min(256, c["SEQ"] // 2)
    d["NTC"] = c["SEQ"] // d["TC"]
    d["NRC"] = c["R"] // 128
    d["NQC"] = (c["E"] * c["DK"]) // 128
    return d


class Sem:
    def __init__(self, h):
        self.h = h
        self.n = 0


class P:

    def __init__(self, nc, role, eng, sems):
        self.nc, self.role, self.eng, self.sems = nc, role, eng, sems

    def _emit(self, _role, _opname, *a, **kw):
        if self.role == _role:
            return getattr(self.eng, _opname)(*a, **kw)
        return None

    def op(self, _role, _opname, *a, sem=None, inc=1, **kw):
        ins = self._emit(_role, _opname, *a, **kw)
        if sem is not None:
            sem.n += inc
            if ins is not None:
                ins.then_inc(sem.h, inc)
        return ins

    def v(self, opname, *a, sem=None, **kw):
        return self.op("vector", opname, *a, sem=sem, **kw)

    def a(self, opname, *a, sem=None, **kw):
        return self.op("scalar", opname, *a, sem=sem, **kw)

    def t(self, opname, *a, sem=None, **kw):
        return self.op("tensor", opname, *a, sem=sem, **kw)

    def p(self, opname, *a, sem=None, **kw):
        return self.op("gpsimd", opname, *a, sem=sem, **kw)

    def ydma(self, dst, src, sem):
        return self.op("sync", "dma_start", sem=sem, inc=16, out=dst, in_=src)

    def pdma(self, dst, src, sem):
        return self.op("gpsimd", "dma_start", sem=sem, inc=16, out=dst, in_=src)

    def pgather(self, dst, src, idxap, eoff, sem):
        ins = None
        if self.role == "gpsimd":
            ins = self.eng.indirect_dma_start(
                out=dst, out_offset=None, in_=src,
                in_offset=IndirectOffsetOnAxis(ap=idxap, axis=0),
                element_offset=eoff)
        sem.n += 16
        if ins is not None:
            ins.then_inc(sem.h, 16)
        return ins

    def wait(self, role, sem, val=None):
        if self.role == role:
            self.eng.wait_ge(sem.h, sem.n if val is None else val)

    def waitall(self, sem, val=None):
        self.eng.wait_ge(sem.h, sem.n if val is None else val)

    def drain(self, role, sem=None):
        ins = self._emit(role, "drain")
        if sem is not None:
            sem.n += 1
            if ins is not None:
                ins.then_inc(sem.h, 1)
        return ins

    def barrier(self):
        for s in ("L", "L2", "LB", "G", "G2", "GS", "GS2"):
            if self.sems[s].n:
                self.waitall(self.sems[s])
        for role, s in (("vector", "BV"), ("scalar", "BA"),
                        ("tensor", "BT"), ("gpsimd", "BP")):
            self.drain(role, self.sems[s])
        for s in ("BV", "BA", "BT", "BP"):
            self.waitall(self.sems[s])


def build_program(C, taps=()):
    c = C
    DM, SEQ, DIN, R, N, K = c["DM"], c["SEQ"], c["DIN"], c["R"], c["N"], c["K"]
    E, DK, ER = c["E"], c["DK"], c["ER"]
    T, DH, NK, NDT, NTS = c["T"], c["DH"], c["NK"], c["NDT"], c["NTS"]
    TW, NTW, TC, NTC, NRC, NQC = c["TW"], c["NTW"], c["TC"], c["NTC"], c["NRC"], c["NQC"]
    EDK = E * DK
    NTC_H = NTC // 2
    BCW = 2 * N
    RQ = R + 2 * N
    nc = bass.Bass("TRN2", target_bir_lowering=False)

    def din(name, shape, dt=F32):
        return nc.dram_tensor(name, shape, dt, kind="ExternalInput").ap()

    x0 = din("x0", [NK, 128, T])
    ffn_wt = din("ffn_wt", [DM, DM])
    ffn_b = din("ffn_b", [128, NK])
    wq_t = din("wq_t", [DM, EDK])
    wk_t = din("wk_t", [DM, EDK])
    af_t = din("af_t", [DM, ER])
    bm2 = din("bm2", [ER, DM])
    ip_x = din("ip_x", [DM, DH])
    ip_z = din("ip_z", [DM, DH])
    convw = din("convw", [128, NDT * K])
    convb = din("convb", [128, NDT])
    xp_t = din("xp_t", [DH, RQ])
    dtp_t = din("dtp_t", [R, DH])
    dtb = din("dtb", [128, NDT])
    alog = din("alog", [128, NDT * N])
    dpar = din("dpar", [128, NDT])
    op_t = din("op_t", [DH, DM])
    dtln = din("dtln", [128, R])
    bcln = din("bcln", [128, 2 * N])
    ident = din("ident", [128, 128])
    ones128 = din("ones128", [128, 1])
    ones1 = din("ones1", [1, 128])
    blk = din("blk", [128, NQC * E])
    r8 = din("r8", [E, 128])
    ixnames = ["ix_rhs0", "ix_rhs1", "ix_pa", "ix_pb", "ix_own", "ix_gsrc", "ix_peer"]
    ixd = {nm: din(nm, [128, 1], U32) for nm in ixnames}

    out = nc.dram_tensor("out", [NK, 128, T], F32, kind="ExternalOutput").ap()
    tap_d = {nm: nc.dram_tensor("tap_" + nm, shp, F32, kind="ExternalOutput").ap()
             for nm, shp in taps}

    def dtmp(name, shape, dt=F32):
        return nc.dram_tensor(name, shape, dt).ap()

    x1d = dtmp("x1d", [NK, 128, T])
    x2d = dtmp("x2d", [NK, 128, T])
    xcd = dtmp("xcd", [2, NDT, 128, T])
    szd = dtmp("szd", [2, NDT, 128, T])
    yd = dtmp("yd", [NDT, 128, SEQ])
    pard = dtmp("pard", [2, NK, 128, T])
    bcd = dtmp("bcd", [BCW, SEQ])
    agx_in = dtmp("agx_in", [2, NK, 128, T], BF16)
    agx_out = dtmp("agx_out", [NCORES, 2, NK, 128, T], BF16)
    w16 = {nm: dtmp("w16_" + nm, [2] + shp, BF16) for nm, shp in [
        ("ffn", [DM, DM]), ("wq", [DM, EDK]), ("wk", [DM, EDK]),
        ("af", [DM, ER]), ("bm2", [ER, DM]),
        ("ipx", [DM, DH]), ("ipz", [DM, DH]), ("op", [DH, DM])]}
    agp_in = dtmp("agp_in", [2 * NTS, 128, RQ])
    agp_out = dtmp("agp_out", [NCORES, 2 * NTS, 128, RQ])
    agg_in = dtmp("agg_in", [NK, 128, T])
    agg_out = dtmp("agg_out", [NCORES, NK, 128, T])

    SCAN_DTRT = 0
    SCAN_NEED = NRC * SEQ + N * TC + 9 * TC + 512 + 4 * N * TC
    BIGW = max(NK * T, SCAN_NEED)
    WW = max(9 * T, 4 * R + 2 * BCW + 256 + 2 * N * TC)

    with ExitStack() as ctx:
        sb = lambda nm, shp, dt=F32: ctx.enter_context(nc.sbuf_tensor(nm, shp, dt))
        big = sb("big", [128, BIGW])
        slab = sb("slab", [128, 2 * NK * 128])
        work = sb("work", [128, WW])
        s_id = sb("s_id", [128, 128])
        s_ones128 = sb("s_ones128", [128, 1])
        s_ones1 = sb("s_ones1", [1, 128])
        s_blk = sb("s_blk", [128, NQC * E])
        s_r8 = sb("s_r8", [E, 128])
        s_halfb = sb("s_halfb", [128, NK])
        s_convw = sb("s_convw", [128, NDT * K])
        s_convb = sb("s_convb", [128, NDT])
        s_dtb = sb("s_dtb", [128, NDT])
        s_aneg = sb("s_aneg", [128, NDT * N])
        s_dpar = sb("s_dpar", [128, NDT])
        s_dtln = sb("s_dtln", [128, R])
        s_bcln = sb("s_bcln", [128, 2 * N])
        s_cexp = sb("s_cexp", [128, 1])
        s_ix = {nm: sb("s_" + nm, [128, 1], U32) for nm in ixnames}
        s_hst = sb("s_hst", [128, NDT * N])
        s_halo = sb("s_halo", [128, NDT * (K - 1)])
        s_xq = sb("s_xq", [128, 2 * RQ])

        PW = max(TW, RQ)
        pb = [ctx.enter_context(nc.psum_tensor(f"pb{i}", [128, PW], F32))
              for i in range(8)]

        semnames = ["L", "L2", "LB", "G", "G2", "GS", "GS2", "C", "SV", "SA", "ST", "SP",
                    "BV", "BA", "BT", "BP", "FR"]
        sems = {nm: Sem(ctx.enter_context(nc.semaphore(nm))) for nm in semnames}
        block = ctx.enter_context(nc.Block())

        def walk(role, eng):
            for s in sems.values():
                s.n = 0
            p = P(nc, role, eng, sems)
            L, G, C = sems["L"], sems["G"], sems["C"]
            L2, G2, GS = sems["L2"], sems["G2"], sems["GS"]
            GS2, LB = sems["GS2"], sems["LB"]
            GSp = lambda i: (GS, GS2)[i % 2]
            Lp = lambda i: (L, L2)[i % 2]
            Gp = lambda i: (G, G2)[i % 2]
            SV, SA, ST, SP, FR = (sems["SV"], sems["SA"], sems["ST"], sems["SP"],
                                  sems["FR"])
            W = work
            s_x0re = W[:, 7 * T:9 * T]
            s_bm2sl = W[:, 0:256]
            s_qsb = W[:, WW - TW:WW]
            s_w8 = W[0:E, WW - TW - T:WW - TW]
            s_bc = W[0:BCW, 4 * R:4 * R + SEQ]
            s_ssm = W[:, WW - 2 * NTS * RQ:WW]
            s_gt = W[:, 0:max(2 * T, 2 * RQ)]
            s_xa = W[:, 0:2 * T]
            s_xb = W[:, 2 * T:4 * T]
            s_xc2 = W[:, 4 * T:6 * T]
            big16 = big[:, :].bitcast(BF16)
            slab16 = slab[:, :].bitcast(BF16)
            bm16 = W[:, 0:256].bitcast(BF16)

            def xth(k, wo=0, ww=None):
                ww = T if ww is None else ww
                return big16[:, 2 * k * T + wo:2 * k * T + wo + ww]

            def xtl(k, wo=0, ww=None):
                ww = T if ww is None else ww
                return big16[:, (2 * k + 1) * T + wo:(2 * k + 1) * T + wo + ww]

            def tap(name, src_sbuf):
                if name in tap_d:
                    p.wait("gpsimd", SV)
                    p.wait("gpsimd", SA)
                    p.wait("gpsimd", SP)
                    p.pdma(tap_d[name][:], src_sbuf, G)

            for dst, src in [(s_id, ident), (s_ones128, ones128), (s_ones1, ones1),
                             (s_blk, blk), (s_r8, r8), (s_convw, convw),
                             (s_convb, convb), (s_dtb, dtb), (s_dpar, dpar),
                             (s_dtln, dtln), (s_bcln, bcln)]:
                p.ydma(dst[:], src[:], L)
            for nm in ixnames:
                p.ydma(s_ix[nm][:], ixd[nm][:], L)
            p.ydma(s_aneg[:], alog[:], L)
            p.ydma(s_halfb[:], ffn_b[:], L)
            p.wait("vector", L)
            p.v("memset", s_cexp[:], float(1.0 / np.sqrt(DK)))
            p.v("tensor_scalar", out=s_halfb[:], in0=s_halfb[:], scalar1=0.5,
                scalar2=None, op0=ALU.mult)
            p.wait("scalar", L)
            p.a("activation", s_aneg[:], s_aneg[:], AF.Exp, sem=SA)
            p.drain("scalar", SA)
            p.wait("vector", SA)
            p.v("tensor_scalar", out=s_aneg[:], in0=s_aneg[:], scalar1=-1.0,
                scalar2=None, op0=ALU.mult, sem=SV)
            p.barrier()

            CH = NK * 128
            presrc = [("ffn", ffn_wt, DM, DM), ("wq", wq_t, DM, EDK),
                      ("wk", wk_t, DM, EDK), ("af", af_t, DM, ER),
                      ("bm2", bm2, ER, DM), ("ipx", ip_x, DM, DH),
                      ("ipz", ip_z, DM, DH), ("op", op_t, DH, DM)]
            chunks = []
            for nm, src, rows, cols in presrc:
                for r0 in range(0, rows, 128):
                    rr = min(128, rows - r0)
                    for c0 in range(0, cols, CH):
                        cc = min(CH, cols - c0)
                        chunks.append((nm, src, r0, rr, c0, cc))
            ps_sa, ps_sv, ps_g = [], [], []
            for ci, (nm, src, r0, rr, c0, cc) in enumerate(chunks):
                sl = (ci % 2) * CH
                bo = (ci % 2) * 2 * CH
                if ci >= 2:
                    p.wait("sync", SV, ps_sv[ci - 2])
                p.ydma(slab[0:rr, sl:sl + cc], src[r0:r0 + rr, c0:c0 + cc], Lp(ci))
                p.wait("scalar", Lp(ci))
                if ci >= 2:
                    p.wait("scalar", Gp(ci), ps_g[ci - 2])
                p.a("copy", big16[0:rr, bo:bo + cc], slab[0:rr, sl:sl + cc], sem=SA)
                ps_sa.append(SA.n)
                p.wait("vector", SA, ps_sa[ci])
                p.v("tensor_tensor", out=big16[0:rr, bo + CH:bo + CH + cc],
                    in0=slab[0:rr, sl:sl + cc], in1=big16[0:rr, bo:bo + cc],
                    op=ALU.subtract, sem=SV)
                ps_sv.append(SV.n)
                p.wait("gpsimd", SV, ps_sv[ci])
                p.pdma(w16[nm][0, r0:r0 + rr, c0:c0 + cc], big16[0:rr, bo:bo + cc],
                       Gp(ci))
                p.pdma(w16[nm][1, r0:r0 + rr, c0:c0 + cc],
                       big16[0:rr, bo + CH:bo + CH + cc], Gp(ci))
                ps_g.append(Gp(ci).n)
            p.waitall(G)
            p.waitall(G2)
            p.barrier()

            def ln_stats(src_tiles):
                rb_off, mrb_off = 5 * T, 6 * T
                m_off, v_off, t_off = 0, T, 2 * T
                ln_sa, ln_fr = [], []
                for k in range(NK):
                    xs_off = 3 * T if k % 2 == 0 else 7 * T
                    sq_off = 4 * T if k % 2 == 0 else 8 * T
                    if k >= 2:
                        p.wait("sync", SA, ln_sa[k - 2])
                        p.wait("sync", FR, ln_fr[k - 2])
                    p.ydma(W[:, xs_off:xs_off + T], src_tiles(k), Lp(k))
                    p.wait("scalar", Lp(k))
                    p.a("activation", W[:, sq_off:sq_off + T],
                        W[:, xs_off:xs_off + T], AF.Square, sem=SA)
                    ln_sa.append(SA.n)
                    p.wait("tensor", SA, ln_sa[k])
                    for w in range(NTW):
                        st, sp = (k == 0), (k == NK - 1)
                        p.t("matmul", pb[4 + w][0:1, 0:TW], s_ones128[:],
                            W[:, xs_off + w * TW:xs_off + (w + 1) * TW],
                            start=st, stop=sp)
                        p.t("matmul", pb[4 + NTW + w][0:1, 0:TW], s_ones128[:],
                            W[:, sq_off + w * TW:sq_off + (w + 1) * TW],
                            start=st, stop=sp,
                            sem=(FR if w == NTW - 1 else None))
                    ln_fr.append(FR.n)
                p.wait("vector", FR)
                for w in range(NTW):
                    p.v("tensor_scalar", out=W[0:1, m_off + w * TW:m_off + (w + 1) * TW],
                        in0=pb[4 + w][0:1, 0:TW], scalar1=1.0 / DM, scalar2=None,
                        op0=ALU.mult)
                    p.v("tensor_scalar", out=W[0:1, v_off + w * TW:v_off + (w + 1) * TW],
                        in0=pb[4 + NTW + w][0:1, 0:TW], scalar1=1.0 / DM, scalar2=None,
                        op0=ALU.mult)
                p.drain("vector")
                p.v("tensor_tensor", out=W[0:1, t_off:t_off + T],
                    in0=W[0:1, m_off:m_off + T], in1=W[0:1, m_off:m_off + T],
                    op=ALU.mult)
                p.drain("vector")
                p.v("tensor_tensor", out=W[0:1, v_off:v_off + T],
                    in0=W[0:1, v_off:v_off + T], in1=W[0:1, t_off:t_off + T],
                    op=ALU.subtract)
                p.drain("vector")
                p.v("tensor_scalar", out=W[0:1, v_off:v_off + T],
                    in0=W[0:1, v_off:v_off + T], scalar1=1e-5, scalar2=None,
                    op0=ALU.add, sem=SV)
                p.wait("scalar", SV)
                p.a("activation", W[0:1, v_off:v_off + T], W[0:1, v_off:v_off + T],
                    AF.Sqrt, sem=SA)
                p.wait("vector", SA)
                p.v("reciprocal", W[0:1, v_off:v_off + T], W[0:1, v_off:v_off + T])
                p.drain("vector")
                p.v("tensor_tensor", out=W[0:1, t_off:t_off + T],
                    in0=W[0:1, m_off:m_off + T], in1=W[0:1, v_off:v_off + T],
                    op=ALU.mult, sem=SV)
                p.wait("tensor", SV)
                for w in range(NTW):
                    p.t("matmul", pb[0][:, 0:TW], s_ones1[:],
                        W[0:1, v_off + w * TW:v_off + (w + 1) * TW],
                        start=True, stop=True)
                    p.t("matmul", pb[1][:, 0:TW], s_ones1[:],
                        W[0:1, t_off + w * TW:t_off + (w + 1) * TW],
                        start=True, stop=True, sem=ST)
                    p.wait("scalar", ST)
                    p.a("copy", W[:, rb_off + w * TW:rb_off + (w + 1) * TW], pb[0][:, 0:TW])
                    p.a("copy", W[:, mrb_off + w * TW:mrb_off + (w + 1) * TW],
                        pb[1][:, 0:TW], sem=SA)
                    p.wait("tensor", SA)
                return rb_off, mrb_off

            def xn_pass(src_tiles, rb_off, mrb_off, to_ag):
                xp_sv, xp_sp = [], []
                for k in range(NK):
                    xs_off = 3 * T if k % 2 == 0 else 7 * T
                    t1_off = 0 if k % 2 == 0 else T
                    xf_off = 2 * T if k % 2 == 0 else 4 * T
                    if k >= 2:
                        p.wait("sync", SV, xp_sv[k - 2][0])
                    else:
                        p.wait("sync", SA)
                        p.wait("sync", FR)
                        p.wait("sync", ST)
                    p.ydma(W[:, xs_off:xs_off + T], src_tiles(k), Lp(k))
                    p.wait("vector", Lp(k))
                    if k >= 2:
                        p.wait("vector", SP, xp_sp[k - 2][0])
                    else:
                        p.wait("vector", ST)
                    p.v("tensor_tensor", out=W[:, t1_off:t1_off + T],
                        in0=W[:, xs_off:xs_off + T], in1=W[:, rb_off:rb_off + T],
                        op=ALU.mult, sem=SV)
                    sv1 = SV.n
                    p.wait("gpsimd", SV, sv1)
                    if k >= 2:
                        p.wait("gpsimd", SV, xp_sv[k - 2][1])
                    p.p("tensor_tensor", out=W[:, xf_off:xf_off + T],
                        in0=W[:, t1_off:t1_off + T], in1=W[:, mrb_off:mrb_off + T],
                        op=ALU.subtract, sem=SP)
                    sp1 = SP.n
                    p.wait("vector", SP, sp1)
                    p.v("tensor_copy", xth(k), W[:, xf_off:xf_off + T], sem=SV)
                    sv2 = SV.n
                    p.wait("gpsimd", SV, sv2)
                    p.p("tensor_tensor", out=xtl(k), in0=W[:, xf_off:xf_off + T],
                        in1=xth(k), op=ALU.subtract, sem=SP)
                    sp2 = SP.n
                    xp_sv.append((sv1, sv2))
                    xp_sp.append((sp1, sp2))
                    if to_ag:
                        p.wait("gpsimd", SP, sp2)
                        p.pdma(agx_in[0, k, :, :], xth(k), G)
                        p.pdma(agx_in[1, k, :, :], xtl(k), G)

            qk_off = 0
            sc_off = NQC * T
            mx_off = sc_off + NTS * E
            w128_off = mx_off + NTS + 8
            h_off = w128_off + T
            eh_off = h_off + T
            t1e_off = 6 * T

            def ffn(src_tiles, dst_tiles):
                rb_off, mrb_off = ln_stats(src_tiles)
                xn_pass(src_tiles, rb_off, mrb_off, None)
                p.barrier()
                qk_st_h = []
                for k in range(NK):
                    sl = (k % 2) * 4 * EDK
                    if k >= 2:
                        p.wait("sync", ST, qk_st_h[k - 2])
                    p.ydma(slab16[:, sl:sl + EDK], w16["wq"][0, k * 128:(k + 1) * 128, :], Lp(k))
                    p.ydma(slab16[:, sl + EDK:sl + 2 * EDK],
                           w16["wq"][1, k * 128:(k + 1) * 128, :], Lp(k))
                    p.ydma(slab16[:, sl + 2 * EDK:sl + 3 * EDK],
                           w16["wk"][0, k * 128:(k + 1) * 128, :], Lp(k))
                    p.ydma(slab16[:, sl + 3 * EDK:sl + 4 * EDK],
                           w16["wk"][1, k * 128:(k + 1) * 128, :], Lp(k))
                    p.wait("tensor", Lp(k))
                    for tgt in range(2):
                        for pss in range(3):
                            wi = sl + (2 * tgt + (1 if pss == 1 else 0)) * EDK
                            xf = xth if pss < 2 else xtl
                            for qc in range(NQC):
                                for w in range(NTW):
                                    st_ = (k == 0 and pss == 0)
                                    sp_ = (k == NK - 1 and pss == 2)
                                    last = (tgt == 1 and pss == 2 and qc == NQC - 1
                                            and w == NTW - 1)
                                    p.t("matmul", pb[tgt * NQC * NTW + qc * NTW + w][:, 0:TW],
                                        slab16[:, wi + qc * 128:wi + (qc + 1) * 128],
                                        xf(k, w * TW, TW),
                                        start=st_, stop=sp_,
                                        sem=(ST if last else None))
                    qk_st_h.append(ST.n)
                p.wait("scalar", ST)
                p.wait("vector", ST)
                for qc in range(NQC):
                    for w in range(NTW):
                        p.a("copy", s_qsb[:, :], pb[qc * NTW + w][:, 0:TW], sem=SA)
                        p.wait("vector", SA)
                        p.v("tensor_tensor",
                            out=W[:, qk_off + qc * T + w * TW:
                                  qk_off + qc * T + (w + 1) * TW],
                            in0=s_qsb[:, :], in1=pb[NQC * NTW + qc * NTW + w][:, 0:TW],
                            op=ALU.mult, sem=SV)
                        p.wait("scalar", SV)
                p.wait("tensor", SV)
                for ts in range(NTS):
                    bank = ts % 2
                    if ts >= 2:
                        p.wait("tensor", SA)
                    for qc in range(NQC):
                        p.t("matmul", pb[bank][:, 0:E],
                            W[:, qk_off + qc * T + ts * 128:
                              qk_off + qc * T + (ts + 1) * 128],
                            s_blk[:, qc * E:(qc + 1) * E],
                            start=(qc == 0), stop=(qc == NQC - 1),
                            sem=(ST if qc == NQC - 1 else None))
                    p.wait("scalar", ST)
                    p.a("copy", W[:, sc_off + ts * E:sc_off + (ts + 1) * E],
                        pb[bank][:, 0:E], sem=SA)
                p.wait("vector", SA)
                sc3 = W[:, sc_off:sc_off + NTS * E].rearrange("p (s e) -> p s e", e=E)
                p.v("tensor_reduce", W[:, mx_off:mx_off + NTS], sc3, axis=AX.X,
                    op=ALU.max)
                p.drain("vector")
                mxb = W[:, mx_off:mx_off + NTS].rearrange(
                    "p s -> p s ()").broadcast_to((128, NTS, E))
                p.v("tensor_tensor", out=sc3, in0=sc3, in1=mxb, op=ALU.subtract)
                p.drain("vector", SV)
                p.wait("scalar", SV)
                p.a("activation", W[:, sc_off:sc_off + NTS * E],
                    W[:, sc_off:sc_off + NTS * E], AF.Exp, scale=s_cexp[:, 0:1],
                    sem=SA)
                p.wait("vector", SA)
                p.v("tensor_reduce", W[:, mx_off:mx_off + NTS], sc3, axis=AX.X,
                    op=ALU.add)
                p.drain("vector")
                p.v("reciprocal", W[:, mx_off:mx_off + NTS], W[:, mx_off:mx_off + NTS])
                p.drain("vector")
                p.v("tensor_tensor", out=sc3, in0=sc3, in1=mxb, op=ALU.mult, sem=SV)
                p.wait("tensor", SV)
                for ts in range(NTS):
                    bank = ts % 2
                    if ts >= 2:
                        p.wait("tensor", SA)
                    p.t("transpose", pb[bank][0:E, 0:128],
                        W[:, sc_off + ts * E:sc_off + (ts + 1) * E], s_id[:], sem=ST)
                    p.wait("scalar", ST)
                    p.a("copy", s_w8[0:E, ts * 128:(ts + 1) * 128],
                        pb[bank][0:E, 0:128], sem=SA)
                p.wait("tensor", SA)
                for w in range(NTW):
                    p.t("matmul", pb[2 + (w % 2)][:, 0:TW], s_r8[:],
                        s_w8[0:E, w * TW:(w + 1) * TW], start=True, stop=True, sem=ST)
                    p.wait("scalar", ST)
                    p.a("copy", W[:, w128_off + w * TW:w128_off + (w + 1) * TW],
                        pb[2 + (w % 2)][:, 0:TW], sem=SA)
                    p.wait("tensor", SA)
                p.wait("sync", SA)
                p.wait("sync", ST)
                p.ydma(slab16[:, 0:ER * NK].rearrange("p (k c) -> p k c", c=ER),
                       w16["af"][0, :, :].rearrange("(k p) c -> p k c", p=128), L)
                p.ydma(slab16[:, ER * NK:2 * ER * NK].rearrange(
                    "p (k c) -> p k c", c=ER),
                    w16["af"][1, :, :].rearrange("(k p) c -> p k c", p=128), L)
                p.wait("tensor", L)
                for w in range(NTW):
                    for pss in range(3):
                        wb = 0 if pss != 1 else ER * NK
                        xf = xth if pss < 2 else xtl
                        for k in range(NK):
                            p.t("matmul", pb[4 + (w % 2)][:, 0:TW],
                                slab16[:, wb + k * ER:wb + k * ER + 128],
                                xf(k, w * TW, TW),
                                start=(k == 0 and pss == 0),
                                stop=(k == NK - 1 and pss == 2),
                                sem=(ST if (k == NK - 1 and pss == 2) else None))
                    p.wait("scalar", ST)
                    p.a("copy", W[:, h_off + w * TW:h_off + (w + 1) * TW],
                        pb[4 + (w % 2)][:, 0:TW], sem=SA)
                p.wait("vector", SA)
                p.v("tensor_tensor", out=W[:, eh_off:eh_off + T],
                    in0=W[:, h_off:h_off + T], in1=W[:, w128_off:w128_off + T],
                    op=ALU.mult)
                p.drain("vector")
                eh16 = W[:, h_off:h_off + T].bitcast(BF16)
                p.v("tensor_copy", eh16[:, 0:T], W[:, eh_off:eh_off + T], sem=SV)
                p.wait("gpsimd", SV)
                p.p("tensor_tensor", out=eh16[:, T:2 * T],
                    in0=W[:, eh_off:eh_off + T], in1=eh16[:, 0:T],
                    op=ALU.subtract, sem=SP)
                p.barrier()
                st_h, sv_epi = [], []
                for o in range(NK):
                    sl = (o % 2) * 2 * NK * 128
                    bs = (o % 2) * 2
                    if o >= 2:
                        p.wait("sync", ST, st_h[o - 2])
                        p.wait("sync", SV, sv_epi[o - 2])
                    p.ydma(slab16[:, sl:sl + NK * 128].rearrange(
                        "p (k c) -> p k c", c=128),
                        w16["ffn"][0, :, o * 128:(o + 1) * 128].rearrange(
                            "(k p) c -> p k c", p=128), Lp(o))
                    p.ydma(slab16[:, sl + NK * 128:sl + 2 * NK * 128].rearrange(
                        "p (k c) -> p k c", c=128),
                        w16["ffn"][1, :, o * 128:(o + 1) * 128].rearrange(
                            "(k p) c -> p k c", p=128), Lp(o))
                    p.ydma(bm16[:, (o % 2) * 256:(o % 2) * 256 + 128],
                           w16["bm2"][0, :, o * 128:(o + 1) * 128], Lp(o))
                    p.ydma(bm16[:, (o % 2) * 256 + 128:(o % 2) * 256 + 256],
                           w16["bm2"][1, :, o * 128:(o + 1) * 128], Lp(o))
                    p.ydma(s_x0re[:, (o % 2) * T:(o % 2) * T + T], src_tiles(o), Lp(o))
                    p.wait("tensor", Lp(o))
                    if o >= 2:
                        p.wait("tensor", SV, sv_epi[o - 2])
                    for pss in range(3):
                        bwi = (o % 2) * 256 + (128 if pss == 1 else 0)
                        eo = 0 if pss < 2 else T
                        for w in range(NTW):
                            p.t("matmul", pb[bs + w][:, 0:TW],
                                bm16[:, bwi:bwi + 128],
                                eh16[:, eo + w * TW:eo + (w + 1) * TW],
                                start=(pss == 0), stop=False)
                    for k in range(NK):
                        for pss in range(3):
                            wb = sl + (NK * 128 if pss == 1 else 0)
                            xf = xth if pss < 2 else xtl
                            for w in range(NTW):
                                p.t("matmul", pb[bs + w][:, 0:TW],
                                    slab16[:, wb + k * 128:wb + (k + 1) * 128],
                                    xf(k, w * TW, TW),
                                    start=False, stop=(k == NK - 1 and pss == 2),
                                    sem=(ST if (k == NK - 1 and pss == 2
                                                and w == NTW - 1) else None))
                    st_h.append(ST.n)
                    p.wait("vector", ST, st_h[o])
                    if o >= 1:
                        p.wait("vector", G)
                    for w in range(NTW):
                        p.v("scalar_tensor_tensor",
                            out=W[:, t1e_off + w * TW:t1e_off + (w + 1) * TW],
                            in0=pb[bs + w][:, 0:TW], scalar=0.5,
                            in1=s_x0re[:, (o % 2) * T + w * TW:
                                       (o % 2) * T + (w + 1) * TW],
                            op0=ALU.mult, op1=ALU.add)
                    p.drain("vector")
                    p.v("tensor_scalar", out=W[:, t1e_off:t1e_off + T],
                        in0=W[:, t1e_off:t1e_off + T],
                        scalar1=s_halfb[:, o:o + 1], scalar2=None, op0=ALU.add,
                        sem=SV)
                    sv_epi.append(SV.n)
                    p.wait("gpsimd", SV, sv_epi[o])
                    p.pdma(dst_tiles(o), W[:, t1e_off:t1e_off + T], G)
                p.barrier()

            x0_t = lambda k: x0[k, :, :]
            x1_t = lambda k: x1d[k, :, :]
            x2_t = lambda k: x2d[k, :, :]
            out_t = lambda k: out[k, :, :]

            ffn(x0_t, x1_t)

            rb1, mrb1 = ln_stats(x1_t)
            xn_pass(x1_t, rb1, mrb1, lambda k: agx_in[k, :, :])
            p.wait("gpsimd", G)
            p.op("gpsimd", "collective_compute", "AllGather", ALU.bypass,
                 replica_groups=[list(range(NCORES))],
                 ins=[agx_in.rearrange("a k p t -> (a k p) t")],
                 outs=[agx_out.rearrange("r a k p t -> (r a k p) t")], sem=C)
            p.waitall(C)
            p.barrier()

            agxv = agx_out.rearrange("r a k p t -> (r a k p) t")
            xpad_off = 0
            xc_off = T + K - 1 + 61
            acc1_off = xc_off + T
            acc2_off = acc1_off + T
            for th in range(2):
                ixr = s_ix["ix_rhs0"] if th == 0 else s_ix["ix_rhs1"]
                for k in range(NK):
                    p.pgather(xth(k), agxv[:, :], ixr[:, 0:1], k * 128 * T, G)
                    p.pgather(xtl(k), agxv[:, :], ixr[:, 0:1],
                              (NK + k) * 128 * T, G)
                p.waitall(G)
                ip_st_h, ip_sa_h = [], []
                pend_ssm = None

                def ssm_block(scb, sa_val):
                    p.wait("tensor", SA, sa_val)
                    p.wait("tensor", LB)
                    ngrp = (NTS + 3) // 4
                    for g in range(ngrp):
                        t0g, t1g = g * 4, min((g + 1) * 4, NTS)
                        if g >= 1:
                            p.wait("tensor", SV)
                        for ts in range(t0g, t1g):
                            p.t("matmul", pb[4 + ts % 4][:, 0:RQ],
                                W[:, xc_off + ts * 128:xc_off + (ts + 1) * 128],
                                s_xq[:, (scb % 2) * RQ:(scb % 2) * RQ + RQ],
                                start=True, stop=True,
                                sem=(FR if ts == t1g - 1 else None))
                        p.wait("vector", FR)
                        for ts in range(t0g, t1g):
                            so = (th * NTS + ts) * RQ
                            if scb == 0:
                                p.v("tensor_copy", s_ssm[:, so:so + RQ],
                                    pb[4 + ts % 4][:, 0:RQ],
                                    sem=(SV if ts == t1g - 1 else None))
                            else:
                                p.v("tensor_tensor", out=s_ssm[:, so:so + RQ],
                                    in0=s_ssm[:, so:so + RQ],
                                    in1=pb[4 + ts % 4][:, 0:RQ],
                                    op=ALU.add, sem=(SV if ts == t1g - 1 else None))

                for cb in range(2 * NDT):
                    isx = cb < NDT
                    di = cb if isx else cb - NDT
                    sl = (cb % 2) * 2 * NK * 128
                    bs = (cb % 2) * 2
                    w16src = w16["ipx"] if isx else w16["ipz"]
                    if cb >= 2:
                        p.wait("sync", ST, ip_st_h[cb - 2][0])
                    p.ydma(slab16[:, sl:sl + NK * 128].rearrange(
                        "p (k c) -> p k c", c=128),
                        w16src[0, :, di * 128:(di + 1) * 128].rearrange(
                            "(k p) c -> p k c", p=128), Lp(cb))
                    p.ydma(slab16[:, sl + NK * 128:sl + 2 * NK * 128].rearrange(
                        "p (k c) -> p k c", c=128),
                        w16src[1, :, di * 128:(di + 1) * 128].rearrange(
                            "(k p) c -> p k c", p=128), Lp(cb))
                    if isx:
                        p.wait("sync", FR, ip_st_h[cb - 1][1] if cb >= 1 else 0)
                        p.ydma(s_xq[:, (cb % 2) * RQ:(cb % 2) * RQ + RQ],
                               xp_t[di * 128:(di + 1) * 128, :], LB)
                    p.wait("tensor", Lp(cb))
                    if cb >= 2:
                        p.wait("tensor", SA, ip_sa_h[cb - 2])
                    for k in range(NK):
                        for pss in range(3):
                            wb = sl + (NK * 128 if pss == 1 else 0)
                            xf = xth if pss < 2 else xtl
                            for w in range(NTW):
                                p.t("matmul", pb[bs + w][:, 0:TW],
                                    slab16[:, wb + k * 128:wb + (k + 1) * 128],
                                    xf(k, w * TW, TW),
                                    start=(k == 0 and pss == 0),
                                    stop=(k == NK - 1 and pss == 2),
                                    sem=(ST if (k == NK - 1 and pss == 2
                                                and w == NTW - 1) else None))
                    if pend_ssm is not None:
                        ssm_block(*pend_ssm)
                        pend_ssm = None
                    if isx:
                        p.wait("scalar", ST)
                        p.wait("scalar", SP)
                        for w in range(NTW):
                            p.a("copy", W[:, xpad_off + (K - 1) + w * TW:
                                          xpad_off + (K - 1) + (w + 1) * TW],
                                pb[bs + w][:, 0:TW], sem=(SA if w == NTW - 1 else None))
                        ip_sa_h.append(SA.n)
                        p.wait("vector", SA)
                        if th == 0:
                            p.v("memset", W[:, xpad_off:xpad_off + (K - 1)], 0.0)
                        else:
                            p.v("tensor_copy", W[:, xpad_off:xpad_off + (K - 1)],
                                s_halo[:, di * (K - 1):(di + 1) * (K - 1)])
                        p.drain("vector")
                        if th == 0:
                            p.v("tensor_copy",
                                s_halo[:, di * (K - 1):(di + 1) * (K - 1)],
                                W[:, xpad_off + T:xpad_off + T + (K - 1)])
                            p.drain("vector")
                        p.v("tensor_scalar", out=W[:, acc1_off:acc1_off + T],
                            in0=W[:, xpad_off:xpad_off + T],
                            scalar1=s_convw[:, di * K:di * K + 1],
                            scalar2=s_convb[:, di:di + 1],
                            op0=ALU.mult, op1=ALU.add)
                        p.drain("vector")
                        p.v("scalar_tensor_tensor", out=W[:, acc2_off:acc2_off + T],
                            in0=W[:, xpad_off + 1:xpad_off + 1 + T],
                            scalar=s_convw[:, di * K + 1:di * K + 2],
                            in1=W[:, acc1_off:acc1_off + T],
                            op0=ALU.mult, op1=ALU.add)
                        p.drain("vector")
                        p.v("scalar_tensor_tensor", out=W[:, acc1_off:acc1_off + T],
                            in0=W[:, xpad_off + 2:xpad_off + 2 + T],
                            scalar=s_convw[:, di * K + 2:di * K + 3],
                            in1=W[:, acc2_off:acc2_off + T],
                            op0=ALU.mult, op1=ALU.add)
                        p.drain("vector")
                        p.v("scalar_tensor_tensor", out=W[:, acc2_off:acc2_off + T],
                            in0=W[:, xpad_off + 3:xpad_off + 3 + T],
                            scalar=s_convw[:, di * K + 3:di * K + 4],
                            in1=W[:, acc1_off:acc1_off + T],
                            op0=ALU.mult, op1=ALU.add, sem=SV)
                        p.drain("vector", SP)
                        p.wait("scalar", SP)
                        p.wait("scalar", GS)
                        p.wait("scalar", GS2)
                        p.wait("scalar", FR)
                        p.a("activation", W[:, xc_off:xc_off + T],
                            W[:, acc2_off:acc2_off + T], AF.Silu, sem=SA)
                        p.wait("gpsimd", SA)
                        p.pdma(xcd[th, di, :, :], W[:, xc_off:xc_off + T], GSp(cb))
                        pend_ssm = (cb, SA.n)
                    else:
                        p.wait("scalar", ST)
                        p.wait("scalar", SP)
                        p.wait("scalar", GS)
                        p.wait("scalar", GS2)
                        for w in range(NTW):
                            p.a("activation",
                                W[:, acc1_off + w * TW:acc1_off + (w + 1) * TW],
                                pb[bs + w][:, 0:TW], AF.Silu,
                                sem=(SA if w == NTW - 1 else None))
                        ip_sa_h.append(SA.n)
                        p.wait("gpsimd", SA)
                        p.pdma(szd[th, di, :, :], W[:, acc1_off:acc1_off + T],
                               GSp(cb))
                    ip_st_h.append((ST.n, FR.n))
                p.barrier()
            p.wait("gpsimd", SV)
            p.pdma(agp_in.rearrange("s p q -> p s q"),
                   s_ssm[:, :].rearrange("p (s q) -> p s q", q=RQ), G)
            p.waitall(G)
            p.op("gpsimd", "collective_compute", "AllGather", ALU.bypass,
                 replica_groups=[list(range(NCORES))],
                 ins=[agp_in.rearrange("s p q -> (s p) q")],
                 outs=[agp_out.rearrange("r s p q -> (r s p) q")], sem=C)
            p.waitall(C)
            agpv = agp_out.rearrange("r s p q -> (r s p) q")
            for half in range(2):
                ix = s_ix["ix_pa"] if half == 0 else s_ix["ix_pb"]
                for ts2 in range(2 * NTS):
                    go = 0 if half == 0 else RQ
                    p.pgather(s_gt[:, go:go + RQ], agpv[:, :], ix[:, 0:1],
                              ts2 * 128 * RQ, G)
                    p.wait("vector", G)
                    if half == 0:
                        p.v("tensor_copy", s_ssm[:, ts2 * RQ:(ts2 + 1) * RQ],
                            s_gt[:, go:go + RQ], sem=SV)
                    else:
                        p.v("tensor_tensor", out=s_ssm[:, ts2 * RQ:(ts2 + 1) * RQ],
                            in0=s_ssm[:, ts2 * RQ:(ts2 + 1) * RQ],
                            in1=s_gt[:, go:go + RQ], op=ALU.add, sem=SV)
                    p.wait("gpsimd", SV)
            p.barrier()
            tap("ssm", s_ssm[:, :])

            for ts2 in range(2 * NTS):
                so = ts2 * RQ
                p.wait("scalar", SA)
                p.a("activation", W[:, 0:R], s_ssm[:, so:so + R], AF.Square, sem=SA)
                p.wait("vector", SA)
                p.v("tensor_reduce", W[:, R:R + 1], W[:, 0:R], axis=AX.X, op=ALU.add)
                p.drain("vector")
                p.v("tensor_scalar", out=W[:, R:R + 1], in0=W[:, R:R + 1],
                    scalar1=1.0 / R, scalar2=1e-6, op0=ALU.mult, op1=ALU.add, sem=SV)
                p.wait("scalar", SV)
                p.a("activation", W[:, R:R + 1], W[:, R:R + 1], AF.Sqrt, sem=SA)
                p.wait("vector", SA)
                p.v("reciprocal", W[:, R:R + 1], W[:, R:R + 1])
                p.drain("vector")
                p.v("tensor_scalar", out=W[:, 0:R], in0=s_ssm[:, so:so + R],
                    scalar1=W[:, R:R + 1], scalar2=None, op0=ALU.mult, sem=SV)
                p.wait("gpsimd", SV)
                p.p("tensor_tensor", out=W[:, 0:R], in0=W[:, 0:R], in1=s_dtln[:, :],
                    op=ALU.mult, sem=SP)
                for bi in range(2):
                    qo = so + R + bi * N
                    wo = 2 * R
                    p.wait("scalar", SP if bi == 0 else SP)
                    p.a("activation", W[:, wo:wo + N], s_ssm[:, qo:qo + N],
                        AF.Square, sem=SA)
                    p.wait("vector", SA)
                    p.v("tensor_reduce", W[:, R:R + 1], W[:, wo:wo + N],
                        axis=AX.X, op=ALU.add)
                    p.drain("vector")
                    p.v("tensor_scalar", out=W[:, R:R + 1], in0=W[:, R:R + 1],
                        scalar1=1.0 / N, scalar2=1e-6, op0=ALU.mult, op1=ALU.add,
                        sem=SV)
                    p.wait("scalar", SV)
                    p.a("activation", W[:, R:R + 1], W[:, R:R + 1], AF.Sqrt, sem=SA)
                    p.wait("vector", SA)
                    p.v("reciprocal", W[:, R:R + 1], W[:, R:R + 1])
                    p.drain("vector")
                    p.v("tensor_scalar", out=W[:, wo:wo + N],
                        in0=s_ssm[:, qo:qo + N], scalar1=W[:, R:R + 1], scalar2=None,
                        op0=ALU.mult, sem=SV)
                    p.wait("gpsimd", SV)
                    p.p("tensor_tensor",
                        out=W[:, 3 * R + bi * N:3 * R + (bi + 1) * N],
                        in0=W[:, wo:wo + N],
                        in1=s_bcln[:, bi * N:(bi + 1) * N], op=ALU.mult, sem=SP)
                p.wait("tensor", SP)
                for rc in range(NRC):
                    p.t("transpose", pb[6 + rc % 2][:, 0:128],
                        W[:, rc * 128:(rc + 1) * 128], s_id[:], sem=ST)
                    p.wait("scalar", ST)
                    p.a("copy", big[:, SCAN_DTRT + rc * SEQ + ts2 * 128:
                                  SCAN_DTRT + rc * SEQ + (ts2 + 1) * 128],
                        pb[6 + rc % 2][:, 0:128], sem=SA)
                    p.wait("tensor", SA)
                p.t("transpose", pb[6][0:BCW, 0:128], W[:, 3 * R:3 * R + BCW],
                    s_id[:], sem=ST)
                p.wait("scalar", ST)
                p.a("copy", s_bc[:, ts2 * 128:(ts2 + 1) * 128],
                    pb[6][0:BCW, 0:128], sem=SA)
                p.wait("tensor", SA)
                p.wait("vector", SA)
            p.wait("gpsimd", SA)
            p.pdma(bcd[:, :], s_bc[:, :], G)
            p.waitall(G)
            p.waitall(L)
            p.barrier()

            zh_off = NRC * SEQ
            scr = zh_off + N * TC
            zu_off, zdt_off, ze1_off = scr, scr + TC, scr + 2 * TC
            zya_off, yzy_off = scr + 3 * TC, scr + 4 * TC
            zxcs_off = scr + 5 * TC
            zszs_off = scr + 7 * TC
            zdtp_off = scr + 9 * TC
            zbb_off = scr + 9 * TC + 512
            zcb_off = zbb_off + N * TC
            zda_off = zcb_off + N * TC
            zdb_off = zda_off + N * TC
            assert zdb_off + N * TC <= BIGW
            sbb2 = big[:, zbb_off:zbb_off + N * TC]
            scb2 = big[:, zcb_off:zcb_off + N * TC]
            u2_off = zu_off
            p.v("memset", s_hst[:, :], 0.0)
            p.drain("vector", SV)
            NIT = NTC * NDT
            NYV = 4
            h_st = [None] * (NIT + 1)
            h_ln = [None] * (NIT + 1)
            h_da = [None] * NIT
            h_u = [None] * (NIT + 1)
            h_scan = [None] * NIT
            h_y = [None] * NIT
            h_db = [None] * (NIT + 1)
            h_ymp = [None] * NIT

            h_ymv = [None] * NIT

            def sc_bbcb(i2):
                tc2 = i2 // NDT
                if i2 >= 1:
                    p.wait("sync", SP, h_db[i2 - 1])
                    p.wait("sync", SP, h_ymp[i2 - 1])
                    p.wait("sync", SV, h_ymv[i2 - 1])
                for n in range(N):
                    p.ydma(sbb2[:, n * TC:(n + 1) * TC],
                           bcd[n:n + 1, tc2 * TC:(tc2 + 1) * TC]
                           .partition_broadcast(128).squeeze(1), LB)
                    p.ydma(scb2[:, n * TC:(n + 1) * TC],
                           bcd[N + n:N + n + 1, tc2 * TC:(tc2 + 1) * TC]
                           .partition_broadcast(128).squeeze(1), LB)

            def sc_loads(i2):
                tc2, di2 = i2 // NDT, i2 % NDT
                th2, t02 = tc2 // NTC_H, (tc2 % NTC_H) * TC
                par = i2 % 2
                if i2 >= 2:
                    p.wait("sync", SV, h_y[i2 - 2])
                    p.wait("sync", ST, h_st[i2 - 2])
                p.ydma(big[:, zxcs_off + par * TC:zxcs_off + (par + 1) * TC],
                       xcd[th2, di2, :, t02:t02 + TC], Lp(i2))
                p.ydma(big[:, zszs_off + par * TC:zszs_off + (par + 1) * TC],
                       szd[th2, di2, :, t02:t02 + TC], Lp(i2))
                p.ydma(big[:, zdtp_off + par * 256:zdtp_off + par * 256 + 256]
                       .rearrange("p (rc c) -> p rc c", c=128),
                       dtp_t[:, di2 * 128:(di2 + 1) * 128].rearrange(
                           "(rc p) c -> p rc c", p=128), Lp(i2))

            def sc_dtmm(i2):
                tc2, di2 = i2 // NDT, i2 % NDT
                par = i2 % 2
                p.wait("tensor", Lp(i2))
                if i2 >= 2:
                    p.wait("tensor", SA, h_ln[i2 - 1])
                for rc in range(NRC):
                    p.t("matmul", pb[par][:, 0:TC],
                        big[:, zdtp_off + par * 256 + rc * 128:
                            zdtp_off + par * 256 + (rc + 1) * 128],
                        big[:, rc * SEQ + tc2 * TC:rc * SEQ + (tc2 + 1) * TC],
                        start=(rc == 0), stop=(rc == NRC - 1),
                        sem=(ST if rc == NRC - 1 else None))
                h_st[i2] = ST.n

            def sc_softplus(i2):
                di2 = i2 % NDT
                par = i2 % 2
                p.wait("scalar", ST, h_st[i2])
                if i2 >= 1:
                    p.wait("scalar", SV, h_u[i2 - 1])
                p.a("activation", big[:, ze1_off:ze1_off + TC], pb[par][:, 0:TC],
                    AF.Exp, bias=s_dtb[:, di2:di2 + 1])
                p.drain("scalar")
                p.a("activation", big[:, zdt_off:zdt_off + TC],
                    big[:, ze1_off:ze1_off + TC], AF.Ln, bias=1.0, sem=SA)
                h_ln[i2] = SA.n
                p.drain("scalar")

            def sc_u(i2):
                par = i2 % 2
                p.wait("vector", SA, h_ln[i2])
                p.wait("vector", Lp(i2))
                p.v("tensor_tensor", out=big[:, zu_off:zu_off + TC],
                    in0=big[:, zdt_off:zdt_off + TC],
                    in1=big[:, zxcs_off + par * TC:zxcs_off + (par + 1) * TC],
                    op=ALU.mult, sem=SV)
                h_u[i2] = SV.n

            def sc_dbx(i2):
                p.wait("gpsimd", LB)
                p.wait("gpsimd", SV, h_u[i2])
                if i2 >= 1:
                    p.wait("gpsimd", SV, h_scan[i2 - 1])
                for n in range(N):
                    p.p("tensor_tensor",
                        out=big[:, zdb_off + n * TC:zdb_off + (n + 1) * TC],
                        in0=big[:, zu_off:zu_off + TC],
                        in1=sbb2[:, n * TC:(n + 1) * TC], op=ALU.mult,
                        sem=(SP if n == N - 1 else None))
                h_db[i2] = SP.n

            sc_bbcb(0)
            sc_loads(0)
            sc_dtmm(0)
            sc_softplus(0)
            sc_u(0)
            sc_dbx(0)
            for idx in range(NIT):
                tc, dt_i = idx // NDT, idx % NDT
                par = idx % 2
                nxt = idx + 1
                if nxt < NIT:
                    sc_loads(nxt)
                    sc_dtmm(nxt)
                if idx >= 1:
                    p.wait("scalar", SV, h_scan[idx - 1])
                for n in range(N):
                    p.a("activation",
                        big[:, zda_off + n * TC:zda_off + (n + 1) * TC],
                        big[:, zdt_off:zdt_off + TC], AF.Exp,
                        scale=s_aneg[:, dt_i * N + n:dt_i * N + n + 1],
                        sem=(SA if n == N - 1 else None))
                h_da[idx] = SA.n
                if nxt < NIT:
                    sc_softplus(nxt)
                p.wait("vector", SA, h_da[idx])
                p.wait("vector", SP, h_db[idx])
                for n in range(N):
                    p.v("tensor_tensor_scan",
                        big[:, zh_off + n * TC:zh_off + (n + 1) * TC],
                        big[:, zda_off + n * TC:zda_off + (n + 1) * TC],
                        big[:, zdb_off + n * TC:zdb_off + (n + 1) * TC],
                        s_hst[:, dt_i * N + n:dt_i * N + n + 1],
                        ALU.mult, ALU.add)
                p.drain("vector")
                hv = big[:, zh_off:zh_off + N * TC].rearrange(
                    "p (n t) -> p n t", t=TC)
                p.v("tensor_copy", s_hst[:, dt_i * N:(dt_i + 1) * N],
                    hv[:, :, TC - 1])
                p.drain("vector", SV)
                h_scan[idx] = SV.n
                p.wait("vector", LB)
                for n in range(NYV):
                    p.v("tensor_tensor",
                        out=big[:, zh_off + n * TC:zh_off + (n + 1) * TC],
                        in0=big[:, zh_off + n * TC:zh_off + (n + 1) * TC],
                        in1=scb2[:, n * TC:(n + 1) * TC], op=ALU.mult)
                p.drain("vector", SV)
                h_ymv[idx] = SV.n
                p.wait("gpsimd", SV, h_scan[idx])
                p.wait("gpsimd", LB)
                for n in range(NYV, N):
                    p.p("tensor_tensor",
                        out=big[:, zh_off + n * TC:zh_off + (n + 1) * TC],
                        in0=big[:, zh_off + n * TC:zh_off + (n + 1) * TC],
                        in1=scb2[:, n * TC:(n + 1) * TC], op=ALU.mult,
                        sem=(SP if n == N - 1 else None))
                h_ymp[idx] = SP.n
                if nxt < NIT:
                    if nxt % NDT == 0:
                        sc_bbcb(nxt)
                    sc_u(nxt)
                    sc_dbx(nxt)
                p.wait("vector", SP, h_ymp[idx])
                hv2 = big[:, zh_off:zh_off + N * TC].rearrange(
                    "p (n t) -> p t n", t=TC)
                p.v("tensor_reduce", big[:, zya_off:zya_off + TC], hv2, axis=AX.X,
                    op=ALU.add)
                p.drain("vector")
                p.wait("vector", GS)
                p.v("scalar_tensor_tensor", out=big[:, yzy_off:yzy_off + TC],
                    in0=big[:, zxcs_off + par * TC:zxcs_off + (par + 1) * TC],
                    scalar=s_dpar[:, dt_i:dt_i + 1],
                    in1=big[:, zya_off:zya_off + TC], op0=ALU.mult, op1=ALU.add)
                p.drain("vector")
                p.v("tensor_tensor", out=big[:, yzy_off:yzy_off + TC],
                    in0=big[:, yzy_off:yzy_off + TC],
                    in1=big[:, zszs_off + par * TC:zszs_off + (par + 1) * TC],
                    op=ALU.mult, sem=SV)
                h_y[idx] = SV.n
                p.wait("gpsimd", SV, h_y[idx])
                p.pdma(yd[dt_i, :, tc * TC:(tc + 1) * TC],
                       big[:, yzy_off:yzy_off + TC], GS)
            p.waitall(GS)
            p.barrier()

            for th in range(2):
                yl_sp = []
                for k in range(NDT):
                    yso = 6 * T + (k % 2) * T
                    if k >= 2:
                        p.wait("sync", SP, yl_sp[k - 2])
                    p.ydma(W[:, yso:yso + T], yd[k, :, th * T:th * T + T], Lp(k))
                    p.wait("scalar", Lp(k))
                    p.a("copy", xth(k), W[:, yso:yso + T], sem=SA)
                    p.wait("gpsimd", SA)
                    p.p("tensor_tensor", out=xtl(k), in0=W[:, yso:yso + T],
                        in1=xth(k), op=ALU.subtract, sem=SP)
                    yl_sp.append(SP.n)
                p.waitall(SP)
                op_st_h, op_sa_h = [], []
                for dm in range(NK):
                    sl = (dm % 2) * 2 * NK * 128
                    bs = (dm % 2) * 2
                    if dm >= 2:
                        p.wait("sync", ST, op_st_h[dm - 2])
                    p.ydma(slab16[:, sl:sl + NDT * 128].rearrange(
                        "p (k c) -> p k c", c=128),
                        w16["op"][0, :, dm * 128:(dm + 1) * 128].rearrange(
                            "(k p) c -> p k c", p=128), Lp(dm))
                    p.ydma(slab16[:, sl + NDT * 128:sl + 2 * NDT * 128].rearrange(
                        "p (k c) -> p k c", c=128),
                        w16["op"][1, :, dm * 128:(dm + 1) * 128].rearrange(
                            "(k p) c -> p k c", p=128), Lp(dm))
                    p.wait("tensor", Lp(dm))
                    if dm >= 2:
                        p.wait("tensor", SA, op_sa_h[dm - 2])
                    for k in range(NDT):
                        for pss in range(3):
                            wb = sl + (NDT * 128 if pss == 1 else 0)
                            xf = xth if pss < 2 else xtl
                            for w in range(NTW):
                                p.t("matmul", pb[bs + w][:, 0:TW],
                                    slab16[:, wb + k * 128:wb + (k + 1) * 128],
                                    xf(k, w * TW, TW),
                                    start=(k == 0 and pss == 0),
                                    stop=(k == NDT - 1 and pss == 2),
                                    sem=(ST if (k == NDT - 1 and pss == 2
                                                and w == NTW - 1) else None))
                    op_st_h.append(ST.n)
                    p.wait("scalar", ST, op_st_h[dm])
                    p.wait("scalar", G)
                    for w in range(NTW):
                        p.a("copy", W[:, w * TW:(w + 1) * TW], pb[bs + w][:, 0:TW],
                            sem=(SA if w == NTW - 1 else None))
                    op_sa_h.append(SA.n)
                    p.wait("gpsimd", SA, op_sa_h[dm])
                    p.pdma(pard[th, dm, :, :], W[:, 0:T], G)
                p.barrier()

            parv = pard.rearrange("a k p t -> (a k p) t")
            p.wait("gpsimd", G)
            p.wait("gpsimd", G2)
            gg_h = []
            for k in range(NK):
                if k >= 2:
                    p.wait("gpsimd", GSp(k), gg_h[k - 2])
                p.pgather(s_gt[:, (k % 2) * T:(k % 2) * T + T], parv[:, :],
                          s_ix["ix_gsrc"][:, 0:1], k * 128 * T, Gp(k))
                p.wait("gpsimd", Gp(k))
                p.pdma(agg_in[k, :, :], s_gt[:, (k % 2) * T:(k % 2) * T + T], GSp(k))
                gg_h.append(GSp(k).n)
            p.waitall(GS)
            p.waitall(GS2)
            p.op("gpsimd", "collective_compute", "AllGather", ALU.bypass,
                 replica_groups=[list(range(NCORES))],
                 ins=[agg_in.rearrange("k p t -> (k p) t")],
                 outs=[agg_out.rearrange("r k p t -> (r k p) t")], sem=C)
            p.waitall(C)
            p.barrier()

            aggv = agg_out.rearrange("r k p t -> (r k p) t")
            for k in range(NK):
                so = (k % 2) * T
                if k >= 2:
                    p.wait("sync", SP)
                    p.wait("sync", GS)
                    p.wait("gpsimd", SP)
                p.ydma(s_xa[:, so:so + T], x1d[k, :, :], Lp(k))
                p.pgather(s_xb[:, so:so + T], parv[:, :],
                          s_ix["ix_own"][:, 0:1], k * 128 * T, Gp(k))
                p.pgather(s_xc2[:, so:so + T], aggv[:, :],
                          s_ix["ix_peer"][:, 0:1], k * 128 * T, Gp(k))
                p.wait("vector", Lp(k))
                p.wait("vector", Gp(k))
                p.v("tensor_tensor", out=s_xa[:, so:so + T],
                    in0=s_xa[:, so:so + T], in1=s_xb[:, so:so + T],
                    op=ALU.add, sem=SV)
                p.wait("gpsimd", SV)
                p.p("tensor_tensor", out=s_xa[:, so:so + T],
                    in0=s_xa[:, so:so + T], in1=s_xc2[:, so:so + T],
                    op=ALU.add, sem=SP)
                p.wait("gpsimd", SP)
                p.pdma(x2d[k, :, :], s_xa[:, so:so + T], GS)
            p.waitall(GS)
            p.barrier()

            ffn(x2_t, out_t)
            p.waitall(G)

        @block.sync
        def _(eng):
            walk("sync", eng)

        @block.vector
        def _(eng):
            walk("vector", eng)

        @block.scalar
        def _(eng):
            walk("scalar", eng)

        @block.tensor
        def _(eng):
            walk("tensor", eng)

        @block.gpsimd
        def _(eng):
            walk("gpsimd", eng)

    return nc



def make_inputs(C, ins, core):
    c = C
    b, h = core // 2, core % 2
    T, DH, NK, NDT = c["T"], c["DH"], c["NK"], c["NDT"]
    DM, SEQ, R, N, K, E, DK, ER = (c["DM"], c["SEQ"], c["R"], c["N"], c["K"],
                                   c["E"], c["DK"], c["ER"])
    NQC = c["NQC"]
    dlo = h * DH
    f32 = lambda a: np.ascontiguousarray(a, dtype=np.float32)
    m = {}
    hs = ins["L0_attn_hidden_states"]
    m["x0"] = f32(hs[b, h * T:(h + 1) * T, :].T.reshape(NK, 128, T))
    m["ffn_wt"] = f32(ins["L0_ffn_W"].T)
    m["ffn_b"] = f32(ins["L0_ffn_b"].reshape(NK, 128).T)
    m["wq_t"] = f32(ins["L0_ffn_Wq"].T)
    m["wk_t"] = f32(ins["L0_ffn_Wk"].T)
    m["af_t"] = f32(ins["L0_ffn_A"].transpose(2, 0, 1).reshape(DM, ER))
    m["bm2"] = f32(ins["L0_ffn_Bm"].transpose(0, 2, 1).reshape(ER, DM))
    m["ip_x"] = f32(ins["L0_attn_in_proj_w"][:, dlo:dlo + DH])
    m["ip_z"] = f32(ins["L0_attn_in_proj_w"][:, c["DIN"] + dlo:c["DIN"] + dlo + DH])
    m["convw"] = f32(ins["L0_attn_conv_w"][dlo:dlo + DH].reshape(NDT, 128, K)
                     .transpose(1, 0, 2).reshape(128, NDT * K))
    m["convb"] = f32(ins["L0_attn_conv_b"][dlo:dlo + DH].reshape(NDT, 128).T)
    m["xp_t"] = f32(ins["L0_attn_x_proj_w"][:, dlo:dlo + DH].T)
    m["dtp_t"] = f32(ins["L0_attn_dt_proj_w"][dlo:dlo + DH, :].T)
    m["dtb"] = f32(ins["L0_attn_dt_bias"][dlo:dlo + DH].reshape(NDT, 128).T)
    m["alog"] = f32(ins["L0_attn_A_log"][dlo:dlo + DH].reshape(NDT, 128, N)
                    .transpose(1, 0, 2).reshape(128, NDT * N))
    m["dpar"] = f32(ins["L0_attn_D_param"][dlo:dlo + DH].reshape(NDT, 128).T)
    m["op_t"] = f32(ins["L0_attn_out_proj_w"][:, dlo:dlo + DH].T)
    m["dtln"] = f32(np.broadcast_to(ins["L0_attn_dt_ln_w"][None, :], (128, R)))
    m["bcln"] = f32(np.broadcast_to(
        np.concatenate([np.asarray(ins["L0_attn_B_ln_w"]),
                        np.asarray(ins["L0_attn_C_ln_w"])])[None, :], (128, 2 * N)))
    m["ident"] = np.eye(128, dtype=np.float32)
    m["ones128"] = np.ones((128, 1), np.float32)
    m["ones1"] = np.ones((1, 128), np.float32)
    blkm = np.zeros((128, NQC * E), np.float32)
    for kt in range(NQC):
        for pp in range(128):
            n_ = kt * 128 + pp
            blkm[pp, kt * E + n_ // DK] = 1.0
    m["blk"] = blkm
    r8m = np.zeros((E, 128), np.float32)
    rr = ER // E
    for e in range(E):
        r8m[e, e * rr:(e + 1) * rr] = 1.0
    m["r8"] = r8m
    ar = np.arange(128, dtype=np.uint32)[:, None]
    u = np.uint32
    m["ix_rhs0"] = u((2 * b) * 2 * NK * 128) + ar
    m["ix_rhs1"] = u((2 * b + 1) * 2 * NK * 128) + ar
    m["ix_pa"] = u((2 * b) * (2 * c["NTS"]) * 128) + ar
    m["ix_pb"] = u((2 * b + 1) * (2 * c["NTS"]) * 128) + ar
    m["ix_own"] = u(h * NK * 128) + ar
    m["ix_gsrc"] = u((1 - h) * NK * 128) + ar
    m["ix_peer"] = u((core ^ 1) * NK * 128) + ar
    for kk in ["ix_rhs0", "ix_rhs1", "ix_pa", "ix_pb", "ix_own", "ix_gsrc",
               "ix_peer"]:
        m[kk] = np.ascontiguousarray(m[kk], dtype=np.uint32)
    return m


def assemble(C, results):
    T = C["T"]
    full = np.empty((len(results) // 2, C["SEQ"], C["DM"]), np.float32)
    for core in range(len(results)):
        b, h = core // 2, core % 2
        o = results[core]["out"].reshape(C["DM"], T)
        full[b, h * T:(h + 1) * T, :] = o.T
    return full


def kernel(**inputs):
    C = derive(cfg_full())
    nc = build_program(C)
    in_maps = [make_inputs(C, inputs, core) for core in range(NCORES)]
    res = run_bass_kernel_spmd(nc, in_maps, core_ids=list(range(NCORES)))
    return assemble(C, res.results)
